# version 03
# baseline (speedup 1.0000x reference)
import numpy as np
from contextlib import ExitStack
import concourse.bass as bass
import concourse.mybir as mybir

F32 = mybir.dt.float32
BF16 = mybir.dt.bfloat16
AF = mybir.ActivationFunctionType
ALU = mybir.AluOpType
AX = mybir.AxisListType
EPS = 1e-5
D = 1024

ENGINES = ("sync", "gpsimd", "vector", "scalar", "tensor")


class Prog:

    def __init__(self, nc, ctx):
        self.nc = nc
        self.streams = {e: [] for e in ENGINES}
        self.sems = {}
        self.counts = {}
        for e in ENGINES:
            self.sems[e] = ctx.enter_context(nc.semaphore(f"s_{e}"))
            self.sems[e + "_dma"] = ctx.enter_context(nc.semaphore(f"sd_{e}"))
            self.counts[e] = 0
            self.counts[e + "_dma"] = 0

    def step(self, engine, fn, wait=(), inc=True, dma=False):
        key = engine + "_dma" if dma else engine
        stepn = 16 if dma else 1
        ev = None
        if inc:
            self.counts[key] += stepn
            ev = (self.sems[key], self.counts[key], stepn)
        waits = [w for w in wait if w is not None]

        def run(eng, ev=ev, waits=waits, fn=fn):
            for w in waits:
                eng.wait_ge(w[0], w[1])
            inst = fn(eng)
            if ev is not None:
                if inst is None:
                    raise RuntimeError("step with inc=True returned no inst")
                inst.then_inc(ev[0], ev[2])

        self.streams[engine].append(run)
        return ev

    def wait_only(self, engine, wait):
        def run(eng, waits=[w for w in wait if w is not None]):
            for w in waits:
                eng.wait_ge(w[0], w[1])
        self.streams[engine].append(run)

    def build(self):
        with self.nc.Block() as blk:
            for name, dec in (("sync", blk.sync), ("gpsimd", blk.gpsimd),
                              ("vector", blk.vector), ("scalar", blk.scalar),
                              ("tensor", blk.tensor)):
                stream = self.streams[name]
                if not stream:
                    continue

                def section(eng, stream=stream):
                    for fn in stream:
                        fn(eng)
                dec(section)


def emit_ln(p, xt_ap, bufs, evs_in, xn_out_ap):
    dd = xt_ap.shape[-1]
    e_sq = p.step("scalar", lambda sc: sc.activation(bufs["sq"], xt_ap, AF.Square),
                  wait=list(evs_in) + [bufs.get("sq_free")])

    def reds(v):
        v.reduce_sum(bufs["sum"], xt_ap, axis=AX.X)
        return v.reduce_sum(bufs["sumsq"], bufs["sq"], axis=AX.X)
    e_red = p.step("vector", reds, wait=list(evs_in) + [e_sq])

    def stats(g):
        g.tensor_scalar_mul(bufs["mean"], bufs["sum"], 1.0 / dd)
        g.tensor_mul(bufs["msq"], bufs["mean"], bufs["mean"])
        g.tensor_scalar_add(bufs["msq"], bufs["msq"], -EPS)
        g.tensor_scalar_mul(bufs["vpe"], bufs["sumsq"], 1.0 / dd)
        return g.tensor_sub(bufs["vpe"], bufs["vpe"], bufs["msq"])
    e_st = p.step("gpsimd", stats, wait=[e_red])

    def rstd(sc):
        sc.activation(bufs["lnv"], bufs["vpe"], AF.Ln)
        return sc.activation(bufs["rstd"], bufs["lnv"], AF.Exp, scale=-0.5)
    e_rs = p.step("scalar", rstd, wait=[e_st])

    def nmr(g):
        g.tensor_mul(bufs["nmr"], bufs["mean"], bufs["rstd"])
        return g.tensor_scalar_mul(bufs["nmr"], bufs["nmr"], -1.0)
    e_nm = p.step("gpsimd", nmr, wait=[e_rs])

    e_xn = p.step("scalar",
                  lambda sc: sc.activation(xn_out_ap, xt_ap, AF.Identity,
                                           bias=bufs["nmr"], scale=bufs["rstd"]),
                  wait=[e_nm, bufs.get("xn_free")])
    return e_xn, e_red


def emit_moe(nc, ctx, p, io, NTOK=4096, NEXP=32, HEXP=256, GROUP=2048,
             prefix="m"):
    NG = NTOK // GROUP
    T = GROUP // 128
    NDC = D // 128
    NHC = HEXP // 128
    NE = NEXP
    x, y = io["x"], io["y"]
    rw, rb, w1, b1, w2, b2 = io["rw"], io["rb"], io["w1"], io["b1"], io["w2"], io["b2"]
    ident = io["ident"]

    sbctx = ExitStack()
    ctx.enter_context(sbctx)
    sb = lambda name, shape, dt: sbctx.enter_context(
        nc.sbuf_tensor(prefix + name, shape, dt))
    ps_ = lambda name, shape, dt: sbctx.enter_context(
        nc.psum_tensor(prefix + name, shape, dt))

    xacc = sb("_xacc", [128, T, D], F32)
    xnbuf = sb("_xn", [128, 2, D], BF16)
    sqbuf = sb("_sq", [128, 2, D], F32)
    xnT = sb("_xnT", [128, NDC, GROUP], BF16)
    rw_st = sb("_rwst", [NEXP, D], F32)
    rwT = sb("_rwT", [128, NDC * NEXP], F32)
    rbT = sb("_rbT", [1, NEXP], F32)
    idt32 = sb("_idt32", [128, 128], F32)
    idt = sb("_idt", [128, 128], BF16)
    ones1 = sb("_ones1", [1, 128], BF16)
    ones1fb = sb("_ones1fb", [1, 128], F32)
    zrow = sb("_zrow", [1, 128], BF16)
    zrowf = sb("_zrowf", [1, 128], F32)
    ones1f = sb("_ones1f", [1, 128], F32)
    xnf = sb("_xnf", [128, 2, D], F32)
    xnT32 = [sb(f"_xnT32{i}", [128, D], F32) for i in range(2)]
    st = sb("_stats", [128, 8, T], F32)
    lgt = sb("_lgt", [128, T, NEXP], F32)
    mx = sb("_mx", [128, T], F32)
    mx2 = sb("_mx2", [128, T], F32)
    hm = sb("_hm", [128, T, NEXP], F32)
    halfc = sb("_halfc", [128, NEXP], F32)
    jnk = sb("_jnk", [128, 8], F32)
    w1sb = [sb(f"_w1sb{i}", [128, NDC, HEXP], BF16) for i in range(2)]
    w2sb = [sb(f"_w2sb{i}", [128, NHC, D], BF16) for i in range(2)]
    b1sb = [sb(f"_b1sb{i}", [1, HEXP], F32) for i in range(2)]
    b2sb = [sb(f"_b2sb{i}", [1, D], F32) for i in range(2)]
    hsb = [sb(f"_hsb{i}", [128, HEXP], BF16) for i in range(2)]
    hT = [sb(f"_hT{i}", [128, NHC, 128], BF16) for i in range(2)]
    ytmp = [sb(f"_ytmp{i}", [128, D], F32) for i in range(2)]

    psA = [ps_(f"_psA{i}", [128, 512], F32) for i in range(2)]
    psTR = [ps_(f"_psTR{i}", [128, 1024], BF16) for i in range(2)]
    psY = [ps_(f"_psY{i}", [128, D], F32) for i in range(2)]

    def stb(row, t):
        return st[:, row, t:t + 1]

    e_id32 = p.step("sync", lambda s: s.dma_start(out=idt32[:], in_=ident),
                    dma=True)
    e_id = p.step("vector", lambda v: v.tensor_copy(idt[:], idt32[:]),
                  wait=[e_id32])
    e_rw = p.step("sync", lambda s: s.dma_start(out=rw_st[:], in_=rw), dma=True)
    e_rb = p.step("sync",
                  lambda s: s.dma_start(out=rbT[:], in_=rb.unsqueeze(0)),
                  dma=True)
    p.step("gpsimd", lambda g: g.memset(halfc[:], 0.5), inc=False)
    p.step("gpsimd", lambda g: g.memset(ones1[:], 1.0), inc=False)
    p.step("gpsimd", lambda g: g.memset(ones1fb[:], 1.0), inc=False)
    p.step("gpsimd", lambda g: g.memset(zrow[:], 0.0), inc=False)
    p.step("gpsimd", lambda g: g.memset(zrowf[:], 0.0), inc=False)
    e_ones = p.step("gpsimd", lambda g: g.memset(ones1f[:], 1.0))

    def rtr(t_):
        insts = []
        for c in range(NDC):
            insts.append(t_.transpose(psY[0][:, c * NEXP:(c + 1) * NEXP],
                                      rw_st[:, c * 128:(c + 1) * 128],
                                      idt32[0:NEXP, 0:NEXP]))
        return insts[-1]
    e_rtr = p.step("tensor", rtr, wait=[e_id32, e_rw])
    def evrwT(v):
        v.tensor_copy(rwT[:], psY[0][:, 0:NDC * NEXP])
        return v.tensor_copy(jnk[:, 0:1], rwT[:, NDC * NEXP - 1:NDC * NEXP])
    e_rwT = p.step("vector", evrwT, wait=[e_rtr])

    ev_store_prev = None
    ev_ymm = {}
    ev_reluh = {}
    ev_reluy = {}
    ev_vhT = {}
    ev_vacc = {}
    ev_red = {}
    ev_ptr = {}
    ev_vlg = {}
    ev_vhm = {}
    ev_evxnT = {}
    ev_ptr32 = {}
    ev_evx32 = {}

    for g in range(NG):
        xg = x.rearrange("(gg t p) d -> gg p t d", p=128, t=T)[g]
        yg = y.rearrange("(gg t p) d -> gg p t d", p=128, t=T)[g]
        e_x = p.step("sync", lambda s, xg=xg: s.dma_start(out=xacc[:], in_=xg),
                     wait=[ev_store_prev], dma=True)

        for t in range(T):
            par = t % 2
            bufs = dict(sq=sqbuf[:, par, :],
                        sum=stb(0, t), sumsq=stb(1, t), mean=stb(2, t),
                        msq=stb(3, t), vpe=stb(4, t), rstd=stb(5, t),
                        nmr=stb(6, t), lnv=stb(7, t),
                        sq_free=ev_red.get((g, t - 2)),
                        xn_free=ev_ptr.get((g, t - 2)))
            bufs["xn_free"] = ev_ptr32.get((g, t - 2))
            exnf, ered = emit_ln(p, xacc[:, t, :], bufs, [e_x], xnf[:, par, :])
            ev_red[(g, t)] = ered
            exn = p.step("scalar",
                         lambda sc, par=par: sc.activation(
                             xnbuf[:, par, :], xnf[:, par, :], AF.Identity),
                         wait=[exnf, ev_ptr.get((g, t - 2))])

            def tr32(t_, par=par):
                insts = []
                for c in range(NDC):
                    insts.append(t_.transpose(
                        psY[par][:, c * 128:(c + 1) * 128],
                        xnf[:, par, c * 128:(c + 1) * 128], idt32[:]))
                return insts[-1]
            t32_wait = [exnf, ev_evx32.get((g, t - 2))]
            if g == 0 and t < 2:
                t32_wait.append(e_rwT if t == 0 else None)
            if g > 0 and t < 2:
                t32_wait.append(ev_reluy.get((g - 1, NE - 1, T - 2 + par)))
            ev_ptr32[(g, t)] = p.step("tensor", tr32, wait=t32_wait)
            def evx32(v, par=par):
                v.tensor_copy(xnT32[par][:], psY[par][:])
                return v.tensor_copy(jnk[:, 0:1], xnT32[par][:, 1023:1024])
            ev_evx32[(g, t)] = p.step("vector", evx32,
                                      wait=[ev_ptr32[(g, t)]])

            def tr(t_, par=par):
                insts = []
                for c in range(NDC):
                    insts.append(t_.transpose(
                        psTR[par][:, c * 128:(c + 1) * 128],
                        xnbuf[:, par, c * 128:(c + 1) * 128], idt[:]))
                return insts[-1]
            tr_wait = [exn, ev_evxnT.get((g, t - 2))]
            if t < 2 and g > 0:
                tr_wait.append(ev_vhT.get((g - 1, NE - 1, T - 2 + par)))
            ev_ptr[(g, t)] = p.step("tensor", tr, wait=tr_wait)

            def evx(v, par=par, t=t):
                src = psTR[par][:, 0:1024].rearrange("p (c q) -> p c q", q=128)
                v.tensor_copy(xnT[:, :, t * 128:(t + 1) * 128], src)
                return v.tensor_copy(jnk[:, 0:8],
                                     xnT[:, :, t * 128 + 127].squeeze(2)
                                     if False else
                                     xnT[:, :, t * 128 + 127:t * 128 + 128]
                                     .rearrange("p c q -> p (c q)"))
            evw = [ev_ptr[(g, t)]]
            if g > 0 and t == 0:
                evw.append(ev_ymm[(g - 1, NE - 1)])
            ev_evxnT[(g, t)] = p.step("vector", evx, wait=evw)

            def rt(t_, par=par, t=t):
                t_.matmul(psA[par][:, 0:NEXP], ones1f[:], rbT[:],
                          start=True, stop=False)
                for c in range(NDC):
                    t_.matmul(psA[par][:, 0:NEXP],
                              xnT32[par][:, c * 128:(c + 1) * 128],
                              rwT[:, c * NEXP:(c + 1) * NEXP],
                              start=False, stop=False)
                return t_.matmul(psA[par][:, 0:NEXP], zrowf[:], rbT[:],
                                 start=False, stop=True)
            rt_wait = [ev_evx32[(g, t)], e_ones, e_rb, ev_vlg.get((g, t - 2))]
            if t < 2 and g > 0:
                rt_wait.append(ev_reluh.get((g - 1, NE - 1, T - 2 + par)))
            e_prt = p.step("tensor", rt, wait=rt_wait)

            def lg_(v, par=par, t=t):
                v.tensor_copy(lgt[:, t, :], psA[par][:, 0:NEXP])
                return v.reduce_max(mx[:, t:t + 1], lgt[:, t, :], axis=AX.X)
            ev_vlg[(g, t)] = p.step("vector", lg_, wait=[e_prt])

            e_mx2 = p.step("gpsimd",
                           lambda g_, t=t: g_.tensor_copy(mx2[:, t:t + 1],
                                                          mx[:, t:t + 1]),
                           wait=[ev_vlg[(g, t)]])

            ev_vhm[(g, t)] = p.step(
                "vector",
                lambda v, t=t: v.scalar_tensor_tensor(
                    hm[:, t, :], lgt[:, t, :], mx2[:, t:t + 1], halfc[:],
                    ALU.is_equal, ALU.mult),
                wait=[e_mx2])

        for e in range(NE):
            wp = e % 2
            if e >= 2:
                wfree = ev_ymm.get((g, e - 2))
            elif g > 0:
                wfree = ev_ymm.get((g - 1, NE - 2 + wp))
            else:
                wfree = None

            p.step("gpsimd",
                   lambda g_, e=e, wp=wp: g_.dma_start(
                       out=w1sb[wp][:],
                       in_=w1[e].rearrange("(c p) h -> p c h", p=128)),
                   wait=[wfree], dma=True)
            e_wl = p.step("gpsimd",
                          lambda g_, e=e, wp=wp: g_.dma_start(
                              out=w2sb[wp][:],
                              in_=w2[e].rearrange("(c p) d -> p c d", p=128)),
                          dma=True)
            p.step("sync",
                   lambda s_, e=e, wp=wp: s_.dma_start(out=b1sb[wp][:],
                                                       in_=b1[e:e + 1, :]),
                   dma=True)
            e_wl2 = p.step("sync",
                           lambda s_, e=e, wp=wp: s_.dma_start(
                               out=b2sb[wp][:], in_=b2[e:e + 1, :]),
                           dma=True)

            for t in range(T):
                par = t % 2

                def hmm(t_, e=e, t=t, par=par, wp=wp):
                    t_.matmul(psA[par][:, 0:HEXP], ones1f[:], b1sb[wp][:],
                              start=True, stop=False)
                    for c in range(NDC):
                        t_.matmul(psA[par][:, 0:HEXP],
                                  xnT[:, c, t * 128:(t + 1) * 128],
                                  w1sb[wp][:, c, :], start=False, stop=False)
                    return t_.matmul(psA[par][:, 0:HEXP], zrowf[:],
                                     b1sb[wp][:], start=False, stop=True)
                if t >= 2:
                    pafree = ev_reluh.get((g, e, t - 2))
                elif e >= 1:
                    pafree = ev_reluh.get((g, e - 1, T - 2 + par))
                else:
                    pafree = ev_vlg.get((g, T - 2 + par))
                hw = [e_wl, e_wl2, pafree]
                if e == 0 and t == 0:
                    hw.append(ev_vhm[(g, T - 1)])
                e_hmm = p.step("tensor", hmm, wait=hw)

                if t >= 2:
                    hsfree = ev_vhT.get((g, e, t - 2))
                elif e >= 1:
                    hsfree = ev_vhT.get((g, e - 1, T - 2 + par))
                elif g > 0:
                    hsfree = ev_vhT.get((g - 1, NE - 1, T - 2 + par))
                else:
                    hsfree = None
                ev_reluh[(g, e, t)] = p.step(
                    "scalar",
                    lambda sc, par=par: sc.activation(hsb[par][:],
                                                      psA[par][:, 0:HEXP],
                                                      AF.Relu),
                    wait=[e_hmm, hsfree])

                def htr(t_, par=par):
                    insts = []
                    for c in range(NHC):
                        insts.append(t_.transpose(
                            psTR[par][:, c * 128:(c + 1) * 128],
                            hsb[par][:, c * 128:(c + 1) * 128], idt[:]))
                    return insts[-1]
                if t >= 2:
                    ptrfree = ev_vhT.get((g, e, t - 2))
                elif e >= 1:
                    ptrfree = ev_vhT.get((g, e - 1, T - 2 + par))
                else:
                    ptrfree = ev_evxnT.get((g, T - 2 + par))
                e_phtr = p.step("tensor", htr,
                                wait=[ev_reluh[(g, e, t)], ptrfree])

                def evh(v, par=par):
                    src = psTR[par][:, 0:HEXP].rearrange("p (c q) -> p c q",
                                                         q=128)
                    v.tensor_copy(hT[par][:], src)
                    return v.tensor_copy(jnk[:, 0:NHC],
                                         hT[par][:, :, 127:128]
                                         .rearrange("p c q -> p (c q)"))
                ev_vhT[(g, e, t)] = p.step("vector", evh, wait=[e_phtr])

                def ymm(t_, e=e, t=t, par=par, wp=wp):
                    last = None
                    for n in range(2):
                        t_.matmul(psY[par][:, n * 512:(n + 1) * 512],
                                  ones1f[:],
                                  b2sb[wp][:, n * 512:(n + 1) * 512],
                                  start=True, stop=False)
                        for c in range(NHC):
                            last = t_.matmul(psY[par][:, n * 512:(n + 1) * 512],
                                             hT[par][:, c, :],
                                             w2sb[wp][:, c, n * 512:(n + 1) * 512],
                                             start=False, stop=(c == NHC - 1))
                    return last
                if t >= 2:
                    pyfree = ev_reluy.get((g, e, t - 2))
                elif e >= 1:
                    pyfree = ev_reluy.get((g, e - 1, T - 2 + par))
                else:
                    pyfree = ev_evx32.get((g, T - 2 + par))
                ev_ymm[(g, e)] = e_ymm = p.step("tensor", ymm,
                                                wait=[ev_vhT[(g, e, t)], pyfree])

                if t >= 2:
                    ytfree = ev_vacc.get((g, e, t - 2))
                elif e >= 1:
                    ytfree = ev_vacc.get((g, e - 1, T - 2 + par))
                elif g > 0:
                    ytfree = ev_vacc.get((g - 1, NE - 1, T - 2 + par))
                else:
                    ytfree = None
                ev_reluy[(g, e, t)] = p.step(
                    "scalar",
                    lambda sc, par=par: sc.activation(ytmp[par][:], psY[par][:],
                                                      AF.Relu),
                    wait=[e_ymm, ytfree])

                ev_vacc[(g, e, t)] = p.step(
                    "vector",
                    lambda v, e=e, t=t, par=par: v.scalar_tensor_tensor(
                        xacc[:, t, :], ytmp[par][:], hm[:, t, e:e + 1],
                        xacc[:, t, :], ALU.mult, ALU.add),
                    wait=[ev_reluy[(g, e, t)]])

        ev_store_prev = p.step(
            "sync", lambda s, yg=yg: s.dma_start(out=yg, in_=xacc[:]),
            wait=[ev_vacc[(g, NE - 1, T - 1)]], dma=True)

    return ev_store_prev


def build_moe(NTOK=4096, NEXP=32, HEXP=256, GROUP=2048):
    nc = bass.Bass("TRN2", target_bir_lowering=False)
    ctx = ExitStack()
    io = {}
    io["x"] = nc.dram_tensor("x", [NTOK, D], F32, kind="ExternalInput").ap()
    io["y"] = nc.dram_tensor("y", [NTOK, D], F32, kind="ExternalOutput").ap()
    io["rw"] = nc.dram_tensor("rw", [NEXP, D], F32, kind="ExternalInput").ap()
    io["rb"] = nc.dram_tensor("rb", [NEXP], F32, kind="ExternalInput").ap()
    io["w1"] = nc.dram_tensor("w1", [NEXP, D, HEXP], F32, kind="ExternalInput").ap()
    io["b1"] = nc.dram_tensor("b1", [NEXP, HEXP], F32, kind="ExternalInput").ap()
    io["w2"] = nc.dram_tensor("w2", [NEXP, HEXP, D], F32, kind="ExternalInput").ap()
    io["b2"] = nc.dram_tensor("b2", [NEXP, D], F32, kind="ExternalInput").ap()
    io["ident"] = nc.dram_tensor("ident", [128, 128], F32,
                                 kind="ExternalInput").ap()
    p = Prog(nc, ctx)
    ev_last = emit_moe(nc, ctx, p, io, NTOK=NTOK, NEXP=NEXP, HEXP=HEXP,
                       GROUP=GROUP)
    p.wait_only("sync", [ev_last])
    p.build()
    ctx.close()
    return nc


def make_ident():
    import ml_dtypes
    return np.eye(128, dtype=ml_dtypes.bfloat16)


I32 = mybir.dt.int32


def emit_quant(p, out_ap, in_ap, qa, evs_in, out_free=None):
    n = in_ap.shape[-1]
    nb = n // 16
    bm = qa["bm"][:, 0:nb]
    sc = qa["sc"][:, 0:nb]
    rsc = qa["rsc"][:, 0:nb]
    rsc2 = qa["rsc2"][:, 0:nb]
    v = qa["v"][:, 0:n]
    u = qa["u"][:, 0:n]
    acc = qa["acc"][:, 0:n]
    bb = qa["bb"][:, 0:n]
    ib = qa["ib"][:, 0:n]
    sg = qa["sg"][:, 0:n]
    in3 = in_ap.rearrange("p (b k) -> p b k", k=16)

    e_bm = p.step("vector",
                  lambda ve: ve.tensor_reduce(bm, in3, axis=AX.X, op=ALU.max,
                                              apply_absolute_value=True),
                  wait=list(evs_in) + [qa.get("free")])
    e_sc = p.step("gpsimd",
                  lambda g: g.tensor_scalar(sc, bm, 1.0 / 6.0, 1e-12,
                                            ALU.mult, ALU.max),
                  wait=[e_bm, qa.get("prev_out")])
    e_rsc = p.step("vector", lambda ve: ve.reciprocal(rsc, sc), wait=[e_sc])
    e_rsc2 = p.step("gpsimd", lambda g: g.tensor_copy(rsc2, rsc), wait=[e_rsc])

    rb = rsc2.unsqueeze(2).broadcast_to([128, nb, 16])
    v3 = v.rearrange("p (b k) -> p b k", k=16)
    e_v = p.step("vector",
                 lambda ve: ve.tensor_tensor(v3, in3, rb, ALU.mult),
                 wait=[e_rsc2])
    e_sg = p.step("scalar", lambda s: s.activation(sg, v, AF.Sign), wait=[e_v])

    def core(ve):
        ve.tensor_mul(u, v, sg)
        ve.tensor_scalar_mul(bb, u, 2.0)
        ve.tensor_copy(ib, bb)
        ve.tensor_scalar(acc, ib, 0.5, 2.0, ALU.mult, ALU.min)
        ve.tensor_copy(ib, u)
        ve.tensor_scalar(bb, ib, 2.0, 4.0, ALU.max, ALU.min)
        ve.tensor_add(acc, acc, bb)
        ve.tensor_scalar_mul(bb, u, 0.5)
        ve.tensor_copy(ib, bb)
        ve.tensor_scalar(bb, ib, 2.0, 4.0, ALU.mult, ALU.max)
        ve.tensor_add(acc, acc, bb)
        return ve.tensor_scalar_add(acc, acc, -6.0)
    e_core = p.step("vector", core, wait=[e_v, e_sg])

    e_m = p.step("vector", lambda ve: ve.tensor_mul(acc, acc, sg),
                 wait=[e_core, e_sg])
    scb = sc.unsqueeze(2).broadcast_to([128, nb, 16])
    acc3 = acc.rearrange("p (b k) -> p b k", k=16)
    out3 = out_ap.rearrange("p (b k) -> p b k", k=16)
    def qout(ve):
        ve.tensor_tensor(out3, acc3, scb, ALU.mult)
        return ve.tensor_copy(qa["rsc"][:, 0:1], out_ap[:, n - 1:n])
    e_out = p.step("vector", qout,
                   wait=[e_m] + ([out_free] if out_free else []))
    qa["prev_out"] = e_out
    return e_out


def emit_ffn1(nc, ctx, p, io, NTOK=4096, I=2816, prefix="f"):
    T = NTOK // 128
    NDC = D // 128
    NIC = I // 128
    ISL = 512
    NSL = (I + ISL - 1) // ISL
    x, y = io["x"], io["y"]
    wg, wu, wd, ident = io["wg"], io["wu"], io["wd"], io["ident"]

    sbctx = ExitStack()
    ctx.enter_context(sbctx)
    sb = lambda name, shape, dt: sbctx.enter_context(
        nc.sbuf_tensor(prefix + name, shape, dt))
    ps_ = lambda name, shape, dt: sbctx.enter_context(
        nc.psum_tensor(prefix + name, shape, dt))

    idt32 = sb("_idt32", [128, 128], F32)
    idt = sb("_idt", [128, 128], BF16)
    wgT = sb("_wgT", [128, NDC, I], BF16)
    wuT = sb("_wuT", [128, NDC, I], BF16)
    wdT = sb("_wdT", [128, NIC, D], BF16)
    wraw = [sb("_wraw", [128, 2816], F32)] * 2
    wq = [sb("_wq", [128, 2816], BF16)] * 2
    qa = dict(bm=sb("_qbm", [128, 32], F32), sc=sb("_qsc", [128, 32], F32),
              rsc=sb("_qrsc", [128, 32], F32), rsc2=sb("_qrsc2", [128, 32], F32),
              v=sb("_qv", [128, 512], F32), u=sb("_qu", [128, 512], F32),
              acc=sb("_qacc", [128, 512], F32), bb=sb("_qbb", [128, 512], F32),
              ib=sb("_qib", [128, 512], I32), sg=sb("_qsg", [128, 512], F32))
    xt = [sb(f"_xt{i}", [128, D], F32) for i in range(2)]
    xnf = [sb(f"_xnf{i}", [128, D], F32) for i in range(2)]
    xq = [sb(f"_xq{i}", [128, D], BF16) for i in range(2)]
    xqT = [sb(f"_xqT{i}", [128, NDC, 128], BF16) for i in range(2)]
    hs = sb("_hs", [128, ISL], F32)
    hh = sb("_hh", [128, ISL], F32)
    hq = sb("_hq", [128, ISL], BF16)
    hqT = sb("_hqT", [128, 4, 128], BF16)
    xout = [sb(f"_xout{i}", [128, D], F32) for i in range(2)]
    jnk = sb("_jnk", [128, 32], F32)
    sqln0 = sb("_sqln", [128, D], F32)
    sqln = sqln0.rearrange("p (g d) -> p g d", g=1).broadcast_to([128, 1, D]) if False else None
    st = sb("_stats", [128, 8, 2], F32)

    psG = ps_("_psG", [128, ISL], F32)
    psU = ps_("_psU", [128, ISL], F32)
    psD = ps_("_psD", [128, D], F32)
    psTR = ps_("_psTR", [128, 1024], BF16)

    def stb(row, par):
        return st[:, row, par:par + 1]

    e_id32 = p.step("sync", lambda s: s.dma_start(out=idt32[:], in_=ident),
                    dma=True)
    e_id = p.step("vector", lambda v: v.tensor_copy(idt[:], idt32[:]),
                  wait=[e_id32])

    ev_wtr_prev = {}
    ev_evw_prev = None
    e_q_prev = None
    chunk_jobs = []
    for wi, (wsrc, wdstT) in enumerate(((wg, wgT), (wu, wuT))):
        for c in range(NIC):
            chunk_jobs.append(("gu", wi, c, wsrc, wdstT))
    for c in range(NDC):
        chunk_jobs.append(("dn", 2, c, wd, wdT))

    ev_wload = {}
    ev_quant = {}
    ev_wtr = {}
    ev_evw = {}
    last_ps_reader = [None]
    for j, (kind, wi, c, wsrc, wdstT) in enumerate(chunk_jobs):
        par = j % 2
        ncols = 1024 if kind == "gu" else 2816
        ev_wload[j] = p.step(
            "sync",
            lambda s, wsrc=wsrc, c=c, par=par, ncols=ncols: s.dma_start(
                out=wraw[par][:, 0:ncols], in_=wsrc[c * 128:(c + 1) * 128, :]),
            wait=[ev_quant.get(j - 1)], dma=True)
        e_q = None
        for s0 in range(0, ncols, 512):
            s1 = min(s0 + 512, ncols)
            e_q = emit_quant(p, wq[par][:, s0:s1], wraw[par][:, s0:s1], qa,
                             [ev_wload[j], e_q_prev],
                             out_free=ev_wtr.get(j - 1))
            e_q_prev = e_q
        ev_quant[j] = e_q

        nblk = ncols // 128
        def wtr(t_, par=par, nblk=nblk):
            insts = []
            for b in range(nblk):
                insts.append(t_.transpose(
                    psTR[:, (b % 8) * 128:((b % 8) + 1) * 128],
                    wq[par][:, b * 128:(b + 1) * 128], idt[:]))
            return insts[-1]
        last_evw = None
        for b0 in range(0, nblk, 8):
            b1 = min(b0 + 8, nblk)
            def wtr8(t_, par=par, b0=b0, b1=b1):
                insts = []
                for b in range(b0, b1):
                    insts.append(t_.transpose(
                        psTR[:, (b - b0) * 128:(b - b0 + 1) * 128],
                        wq[par][:, b * 128:(b + 1) * 128], idt[:]))
                return insts[-1]
            e_tr = p.step("tensor", wtr8,
                          wait=[ev_quant[j], e_id, last_ps_reader[0]])
            def evw(v, kind=kind, wi=wi, c=c, b0=b0, b1=b1, wdstT=wdstT):
                for b in range(b0, b1):
                    src = psTR[:, (b - b0) * 128:(b - b0 + 1) * 128]
                    dst = wdstT[:, b, c * 128:(c + 1) * 128]
                    v.tensor_copy(dst, src)
                return v.tensor_copy(
                    jnk[:, 0:b1 - b0],
                    wdstT[:, b0:b1, c * 128 + 127:c * 128 + 128]
                    .rearrange("p c q -> p (c q)"))
            last_evw = p.step("vector", evw, wait=[e_tr])
            last_ps_reader[0] = last_evw
        ev_wtr[j] = e_tr
        ev_evw[j] = last_evw
    e_wprep = ev_evw[len(chunk_jobs) - 1]

    ev_ln = {}
    ev_lnred = {}
    ev_xq = {}
    ev_xqT = {}
    ev_res = {}
    ev_store = {}
    ev_evxq = {}
    ev_down_done = {}
    e_hqT_ev = None
    e_down_last = None
    e_silu_prev = None
    e_h_prev = None
    e_hq_prev = None
    for t in range(T):
        par = t % 2
        xg = x[t * 128:(t + 1) * 128, :]
        yg = y[t * 128:(t + 1) * 128, :]
        e_x = p.step("sync",
                     lambda s, xg=xg, par=par: s.dma_start(out=xt[par][:], in_=xg),
                     wait=[ev_store.get(t - 2)], dma=True)
        bufs = dict(sq=sqln0[:], sum=stb(0, par), sumsq=stb(1, par),
                    mean=stb(2, par), msq=stb(3, par), vpe=stb(4, par),
                    rstd=stb(5, par), nmr=stb(6, par), lnv=stb(7, par),
                    sq_free=ev_lnred.get(t - 1), xn_free=ev_xq.get(t - 2))
        exn, ered = emit_ln(p, xt[par][:], bufs, [e_x], xnf[par][:])
        ev_ln[t] = exn
        ev_lnred[t] = ered
        e_xq = emit_quant(p, xq[par][:, 0:512], xnf[par][:, 0:512], qa, [exn],
                          out_free=ev_xqT.get(t - 2))
        e_xq = emit_quant(p, xq[par][:, 512:1024], xnf[par][:, 512:1024], qa,
                          [exn], out_free=ev_xqT.get(t - 2))
        ev_xq[t] = e_xq

        def xtr(t_, par=par):
            insts = []
            for c in range(NDC):
                insts.append(t_.transpose(psTR[:, c * 128:(c + 1) * 128],
                                          xq[par][:, c * 128:(c + 1) * 128],
                                          idt[:]))
            return insts[-1]
        e_xtr = p.step("tensor", xtr, wait=[e_xq, last_ps_reader[0]])
        def evxq(v, par=par):
            src = psTR[:, 0:1024].rearrange("p (c q) -> p c q", q=128)
            v.tensor_copy(xqT[par][:], src)
            return v.tensor_copy(jnk[:, 0:NDC],
                                 xqT[par][:, :, 127:128]
                                 .rearrange("p c q -> p (c q)"))
        ev_evxq[t] = p.step("vector", evxq, wait=[e_xtr])
        last_ps_reader[0] = ev_evxq[t]
        ev_xqT[t] = e_xtr

        for si in range(NSL):
            s0 = si * ISL
            s1 = min(s0 + ISL, I)
            w_ = s1 - s0
            def gmm(t_, par=par, s0=s0, w_=w_):
                last = None
                for c in range(NDC):
                    nn = 0
                    while nn < w_:
                        ns = min(512, w_ - nn)
                        last = t_.matmul(psG[:, nn:nn + ns],
                                         xqT[par][:, c, :],
                                         wgT[:, c, s0 + nn:s0 + nn + ns],
                                         start=(c == 0), stop=(c == NDC - 1))
                        nn += ns
                return last
            def umm(t_, par=par, s0=s0, w_=w_):
                last = None
                for c in range(NDC):
                    nn = 0
                    while nn < w_:
                        ns = min(512, w_ - nn)
                        last = t_.matmul(psU[:, nn:nn + ns],
                                         xqT[par][:, c, :],
                                         wuT[:, c, s0 + nn:s0 + nn + ns],
                                         start=(c == 0), stop=(c == NDC - 1))
                        nn += ns
                return last
            e_gmm = p.step("tensor", gmm, wait=[ev_evxq[t], e_silu_prev])
            e_umm = p.step("tensor", umm, wait=[e_h_prev])
            e_silu = p.step("scalar",
                            lambda sc, w_=w_: sc.activation(hs[:, 0:w_],
                                                            psG[:, 0:w_],
                                                            AF.Silu),
                            wait=[e_gmm, e_h_prev])
            e_h = p.step("vector",
                         lambda ve, w_=w_: ve.tensor_tensor(
                             hh[:, 0:w_], hs[:, 0:w_], psU[:, 0:w_], ALU.mult),
                         wait=[e_silu, e_umm, e_hq_prev])
            e_silu_prev = e_silu
            e_h_prev = e_h
            e_hq = emit_quant(p, hq[:, 0:w_], hh[:, 0:w_], qa, [e_h],
                              out_free=e_hqT_ev)
            e_hq_prev = e_hq
            nblk = w_ // 128
            def htr(t_, nblk=nblk):
                insts = []
                for b in range(nblk):
                    insts.append(t_.transpose(
                        psTR[:, b * 128:(b + 1) * 128],
                        hq[:, b * 128:(b + 1) * 128], idt[:]))
                return insts[-1]
            e_htr = p.step("tensor", htr, wait=[e_hq, last_ps_reader[0]])
            def evhq(v, nblk=nblk):
                src = psTR[:, 0:nblk * 128].rearrange("p (c q) -> p c q", q=128)
                v.tensor_copy(hqT[:, 0:nblk, :], src)
                return v.tensor_copy(jnk[:, 0:nblk],
                                     hqT[:, 0:nblk, 127:128]
                                     .rearrange("p c q -> p (c q)"))
            e_hqT_ev = p.step("vector", evhq, wait=[e_htr, e_down_last])
            last_ps_reader[0] = e_hqT_ev

            def dmm(t_, si=si, s0=s0, nblk=nblk):
                last = None
                for b in range(nblk):
                    ic = (s0 // 128) + b
                    for n in range(2):
                        last = t_.matmul(psD[:, n * 512:(n + 1) * 512],
                                         hqT[:, b, :],
                                         wdT[:, ic, n * 512:(n + 1) * 512],
                                         start=(si == 0 and b == 0),
                                         stop=(si == NSL - 1 and b == nblk - 1))
                return last
            e_down_last = p.step("tensor", dmm,
                                 wait=[e_hqT_ev, ev_res.get(t - 1)])
        ev_down_done[t] = e_down_last

        def res(ve, par=par):
            return ve.scalar_tensor_tensor(xout[par][:], psD[:], 0.5,
                                           xt[par][:], ALU.mult, ALU.add)
        ev_res[t] = p.step("vector", res,
                           wait=[e_down_last, ev_store.get(t - 2)])
        ev_store[t] = p.step("sync",
                             lambda s, yg=yg, par=par: s.dma_start(
                                 out=yg, in_=xout[par][:]),
                             wait=[ev_res[t]], dma=True)
    return ev_store[T - 1]


def build_ffn1(NTOK=4096, I=2816):
    nc = bass.Bass("TRN2", target_bir_lowering=False)
    ctx = ExitStack()
    io = {}
    io["x"] = nc.dram_tensor("x", [NTOK, D], F32, kind="ExternalInput").ap()
    io["y"] = nc.dram_tensor("y", [NTOK, D], F32, kind="ExternalOutput").ap()
    io["wg"] = nc.dram_tensor("wg", [I, D], F32, kind="ExternalInput").ap()
    io["wu"] = nc.dram_tensor("wu", [I, D], F32, kind="ExternalInput").ap()
    io["wd"] = nc.dram_tensor("wd", [D, I], F32, kind="ExternalInput").ap()
    io["ident"] = nc.dram_tensor("ident", [128, 128], F32,
                                 kind="ExternalInput").ap()
    p = Prog(nc, ctx)
    ev_last = emit_ffn1(nc, ctx, p, io, NTOK=NTOK, I=I)
    p.wait_only("sync", [ev_last])
    p.build()
    ctx.close()
    return nc


def emit_attn(nc, ctx, p, io, NTOK=4096, NH=16, HD=64, causal=False,
              rope=False, biases=False, prefix="a", debug=False):
    T = NTOK // 128
    NDC = D // 128
    NQB = NTOK // 512
    NKC = NTOK // 128
    x, y, ident = io["x"], io["y"], io["ident"]

    sbctx = ExitStack()
    ctx.enter_context(sbctx)
    sb = lambda name, shape, dt: sbctx.enter_context(
        nc.sbuf_tensor(prefix + name, shape, dt))
    ps_ = lambda name, shape, dt: sbctx.enter_context(
        nc.psum_tensor(prefix + name, shape, dt))
    dram = lambda name, shape, dt: nc.dram_tensor(prefix + name, shape, dt,
                                                  kind="Internal").ap()

    idt32 = sb("_idt32", [128, 128], F32)
    idt = sb("_idt", [128, 128], BF16)
    ones1 = sb("_ones1", [1, 128], BF16)
    ones1fb = sb("_ones1fb", [1, 128], F32)
    zrow = sb("_zrow", [1, 128], BF16)
    zrowf = sb("_zrowf", [1, 128], F32)
    onesf = sb("_onesf", [1, 64], F32)
    jnk = sb("_jnk", [128, 16], F32)
    ones1f = sb("_ones1f", [1, 128], F32)

    wT_h = {w: dram(f"_w{w}T_h", [128, NDC, D], BF16) for w in "qkvo"}
    qT_h = dram("_qT_h", [128, NDC, NTOK], BF16)
    qTlo_h = dram("_qTlo_h", [128, NDC, NTOK], BF16)

    e_id32 = p.step("sync", lambda s: s.dma_start(out=idt32[:], in_=ident),
                    dma=True)
    e_id = p.step("vector", lambda v: v.tensor_copy(idt[:], idt32[:]),
                  wait=[e_id32])
    p.step("gpsimd", lambda g: g.memset(ones1[:], 1.0), inc=False)
    p.step("gpsimd", lambda g: g.memset(ones1f[:], 1.0), inc=False)
    e_onesf = p.step("gpsimd", lambda g: g.memset(onesf[:], 1.0))

    with ExitStack() as prepctx:
        psb = lambda name, shape, dt: prepctx.enter_context(
            nc.sbuf_tensor(prefix + name, shape, dt))
        pps = lambda name, shape, dt: prepctx.enter_context(
            nc.psum_tensor(prefix + name, shape, dt))
        wst = [psb(f"_wst{i}", [128, D], BF16) for i in range(2)]
        tb = [psb(f"_tb{i}", [128, NDC, 128], BF16) for i in range(2)]
        psW = [pps(f"_psW{i}", [128, 1024], BF16) for i in range(2)]
        ev_tr = {}
        ev_ev = {}
        ev_out = {}
        ev_ld = {}
        jobs = [(w, r) for w in "qkvo" for r in range(NDC)]
        for j, (w, r) in enumerate(jobs):
            par = j % 2
            wsrc = io["W" + w]
            ev_ld[j] = p.step(
                "gpsimd",
                lambda g, wsrc=wsrc, r=r, par=par: g.dma_start(
                    out=wst[par][:], in_=wsrc[r * 128:(r + 1) * 128, :]),
                wait=[ev_tr.get(j - 2)], dma=True)

            def wtr(t_, par=par):
                insts = []
                for c in range(NDC):
                    insts.append(t_.transpose(
                        psW[par][:, c * 128:(c + 1) * 128],
                        wst[par][:, c * 128:(c + 1) * 128], idt[:]))
                return insts[-1]
            ev_tr[j] = p.step("tensor", wtr,
                              wait=[ev_ld[j], e_id, ev_ev.get(j - 2)])
            def wev(v, par=par):
                src = psW[par][:].rearrange("p (c q) -> p c q", q=128)
                v.tensor_copy(tb[par][:], src)
                return v.tensor_copy(jnk[:, 0:NDC],
                                     tb[par][:, :, 127:128]
                                     .rearrange("p c q -> p (c q)"))
            ev_ev[j] = p.step("vector", wev, wait=[ev_tr[j], ev_out.get(j - 2)])
            ev_out[j] = p.step(
                "sync",
                lambda s, w=w, r=r, par=par: s.dma_start(
                    out=wT_h[w][:, :, r * 128:(r + 1) * 128], in_=tb[par][:]),
                wait=[ev_ev[j]], dma=True)
        e_prep = [ev_out[len(jobs) - 1], ev_tr[len(jobs) - 1]]

    st = sb("_stats", [128, 8, 2], F32)

    def stb(row, par):
        return st[:, row, par:par + 1]

    if biases:
        bq = sb("_bq", [1, D], F32)
        bk = sb("_bk", [1, D], F32)
        bv = sb("_bv", [1, D], F32)
        bo = sb("_bo", [1, D], F32)
        e_bias = None
        for name, buf in (("bq", bq), ("bk", bk), ("bv", bv), ("bo", bo)):
            e_bias = p.step("sync",
                            lambda s, name=name, buf=buf: s.dma_start(
                                out=buf[:], in_=io[name].unsqueeze(0)),
                            dma=True)
    else:
        bq = bk = bv = bo = None
        e_bias = None

    xnT_h = dram("_xnT_h", [128, NDC, NTOK], BF16)
    evA_last = []
    with ExitStack() as actx:
        asb = lambda name, shape, dt: actx.enter_context(
            nc.sbuf_tensor(prefix + name, shape, dt))
        aps = lambda name, shape, dt: actx.enter_context(
            nc.psum_tensor(prefix + name, shape, dt))
        wqT = asb("_wqT", [128, NDC, D], BF16)
        xtl = [asb(f"_xtl{i}", [128, D], F32) for i in range(2)]
        xnb = [asb(f"_xnb{i}", [128, D], BF16) for i in range(2)]
        sqln = asb("_sqln", [128, 2, D], F32)
        xnTt = [asb(f"_xnTt{i}", [128, NDC, 128], BF16) for i in range(2)]
        qkb = [asb(f"_qkb{i}", [128, D], BF16) for i in range(2)]
        qlob = [asb(f"_qlob{i}", [128, D], BF16) for i in range(2)]
        qTt = [asb(f"_qTt{i}", [128, NDC, 128], BF16) for i in range(2)]
        qTlo = [asb(f"_qTlo{i}", [128, NDC, 128], BF16) for i in range(2)]
        qf = asb("_qf", [128, D], F32)
        if rope:
            rot = asb("_rot", [128, D], F32)
            ctile = [asb(f"_ct{i}", [128, D], F32) for i in range(2)]
            stile = [asb(f"_st{i}", [128, D], F32) for i in range(2)]
        psP = [aps(f"_psP{i}", [128, D], F32) for i in range(2)]
        psTR = [aps(f"_psTR{i}", [128, 1024], BF16) for i in range(2)]

        e_wq = p.step("sync", lambda s: s.dma_start(out=wqT[:], in_=wT_h["q"]),
                      wait=e_prep, dma=True)

        ev_red = {}
        ev_xtr = {}
        ev_evxnT = {}
        ev_xndma = {}
        ev_qmm = {}
        ev_qev = {}
        ev_qtr = {}
        ev_qdma = {}
        ev_qlodma = {}
        last_ps = [None]
        ev_ropeq = {}
        for t in range(T):
            par = t % 2
            e_x = p.step("sync",
                         lambda s, t=t, par=par: s.dma_start(
                             out=xtl[par][:], in_=x[t * 128:(t + 1) * 128, :]),
                         wait=[ev_evxnT.get(t - 2)], dma=True)
            if rope:
                e_c = p.step("sync",
                             lambda s, t=t, par=par: s.dma_start(
                                 out=ctile[par][:],
                                 in_=io["cos"][t * 128:(t + 1) * 128, :]),
                             wait=[ev_ropeq.get(t - 2)], dma=True)
                e_s = p.step("sync",
                             lambda s, t=t, par=par: s.dma_start(
                                 out=stile[par][:],
                                 in_=io["sin"][t * 128:(t + 1) * 128, :]),
                             dma=True)
            else:
                e_c = e_s = None
            bufs = dict(sq=sqln[:, par, :], sum=stb(0, par), sumsq=stb(1, par),
                        mean=stb(2, par), msq=stb(3, par), vpe=stb(4, par),
                        rstd=stb(5, par), nmr=stb(6, par), lnv=stb(7, par),
                        sq_free=ev_red.get(t - 2), xn_free=ev_xtr.get(t - 2))
            exn, ered = emit_ln(p, xtl[par][:], bufs, [e_x], xnb[par][:])
            ev_red[t] = ered

            def xtr(t_, par=par):
                insts = []
                for c in range(NDC):
                    insts.append(t_.transpose(
                        psTR[par][:, c * 128:(c + 1) * 128],
                        xnb[par][:, c * 128:(c + 1) * 128], idt[:]))
                return insts[-1]
            ev_xtr[t] = p.step("tensor", xtr, wait=[exn, last_ps[0]])
            def evxn(v, par=par):
                src = psTR[par][:].rearrange("p (c q) -> p c q", q=128)
                v.tensor_copy(xnTt[par][:], src)
                return v.tensor_copy(jnk[:, 0:NDC],
                                     xnTt[par][:, :, 127:128]
                                     .rearrange("p c q -> p (c q)"))
            ev_evxnT[t] = p.step("vector", evxn,
                                 wait=[ev_xtr[t], ev_xndma.get(t - 2)])
            last_ps[0] = ev_evxnT[t]
            ev_xndma[t] = p.step(
                "sync",
                lambda s, t=t, par=par: s.dma_start(
                    out=xnT_h[:, :, t * 128:(t + 1) * 128], in_=xnTt[par][:]),
                wait=[ev_evxnT[t]], dma=True)

            def mmq(t_, par=par):
                if biases:
                    for n in range(2):
                        t_.matmul(psP[par][:, n * 512:(n + 1) * 512],
                                  ones1f[:], bq[:, n * 512:(n + 1) * 512],
                                  start=True, stop=False)
                last = None
                for c in range(NDC):
                    for n in range(2):
                        last = t_.matmul(psP[par][:, n * 512:(n + 1) * 512],
                                         xnTt[par][:, c, :],
                                         wqT[:, c, n * 512:(n + 1) * 512],
                                         start=(not biases and c == 0),
                                         stop=(c == NDC - 1))
                return last
            e_mm = p.step("tensor", mmq,
                          wait=[ev_evxnT[t], e_wq, e_bias, ev_qev.get(t - 2)])
            if rope:
                def evr(v, par=par):
                    v.tensor_copy(qf[:], psP[par][:])
                    ro = rot[:].rearrange("p (a two) -> p a two", two=2)
                    qfv = qf[:].rearrange("p (a two) -> p a two", two=2)
                    v.tensor_scalar_mul(ro[:, :, 0:1], qfv[:, :, 1:2], -1.0)
                    v.tensor_copy(ro[:, :, 1:2], qfv[:, :, 0:1])
                    v.tensor_mul(qf[:], qf[:], ctile[par][:])
                    v.tensor_mul(rot[:], rot[:], stile[par][:])
                    v.tensor_add(qf[:], qf[:], rot[:])
                    v.tensor_copy(qkb[par][:], qf[:])
                    v.tensor_sub(qlob[par][:], qf[:], qkb[par][:])
                    v.tensor_copy(jnk[:, 0:1], qkb[par][:, 1023:1024])
                    return v.tensor_copy(jnk[:, 1:2], qlob[par][:, 1023:1024])
                e_ev = p.step("vector", evr,
                              wait=[e_mm, e_c, e_s, ev_qtr.get(t - 2)])
                ev_ropeq[t] = e_ev
            else:
                def evq0(v, par=par):
                    v.tensor_copy(qkb[par][:], psP[par][:])
                    v.tensor_tensor(qlob[par][:], psP[par][:],
                                    qkb[par][:], ALU.subtract)
                    v.tensor_copy(jnk[:, 0:1], qkb[par][:, 1023:1024])
                    return v.tensor_copy(jnk[:, 1:2], qlob[par][:, 1023:1024])
                e_ev = p.step("vector", evq0,
                              wait=[e_mm, ev_qtr.get(t - 2)])
            ev_qev[t] = e_ev

            def qtr(t_, par=par):
                insts = []
                for c in range(NDC):
                    insts.append(t_.transpose(
                        psTR[par][:, c * 128:(c + 1) * 128],
                        qkb[par][:, c * 128:(c + 1) * 128], idt[:]))
                return insts[-1]
            e_tr = p.step("tensor", qtr, wait=[e_ev, last_ps[0]])
            ev_qtr[t] = e_tr
            def evq(v, par=par):
                src = psTR[par][:].rearrange("p (c q) -> p c q", q=128)
                v.tensor_copy(qTt[par][:], src)
                return v.tensor_copy(jnk[:, 0:NDC],
                                     qTt[par][:, :, 127:128]
                                     .rearrange("p c q -> p (c q)"))
            e_evq = p.step("vector", evq, wait=[e_tr, ev_qdma.get(t - 2)])
            ev_qdma[t] = p.step(
                "sync",
                lambda s, t=t, par=par: s.dma_start(
                    out=qT_h[:, :, t * 128:(t + 1) * 128], in_=qTt[par][:]),
                wait=[e_evq], dma=True)
            last_ps[0] = e_evq

            def qtrlo(t_, par=par):
                insts = []
                for c in range(NDC):
                    insts.append(t_.transpose(
                        psTR[par][:, c * 128:(c + 1) * 128],
                        qlob[par][:, c * 128:(c + 1) * 128], idt[:]))
                return insts[-1]
            e_trlo = p.step("tensor", qtrlo, wait=[ev_qev[t], last_ps[0]])
            def evqlo(v, par=par):
                src = psTR[par][:].rearrange("p (c q) -> p c q", q=128)
                v.tensor_copy(qTlo[par][:], src)
                return v.tensor_copy(jnk[:, 0:NDC],
                                     qTlo[par][:, :, 127:128]
                                     .rearrange("p c q -> p (c q)"))
            e_evqlo = p.step("vector", evqlo,
                             wait=[e_trlo, ev_qlodma.get(t - 2)])
            ev_qlodma[t] = p.step(
                "sync",
                lambda s, t=t, par=par: s.dma_start(
                    out=qTlo_h[:, :, t * 128:(t + 1) * 128], in_=qTlo[par][:]),
                wait=[e_evqlo], dma=True)
            last_ps[0] = e_evqlo
        evA1 = [ev_qdma[T - 1], ev_qlodma[T - 1], ev_xndma[T - 1], last_ps[0]]

    KT = sb("_KT", [128, NDC, NTOK], BF16)
    Vb = sb("_Vb", [128, NKC, NH * (HD + 2)], BF16)
    ctxT = sb("_ctxT", [128, NDC, 512], BF16)
    e_vones = p.step("gpsimd", lambda g: (g.memset(
        Vb[:].rearrange("p k (h c) -> p k h c", c=HD + 2)[:, :, :, HD:HD + 1],
        1.0), g.memset(
        Vb[:].rearrange("p k (h c) -> p k h c", c=HD + 2)[:, :, :,
                        HD + 1:HD + 2], 0.0))[-1], wait=evA1)

    with ExitStack() as actx:
        asb = lambda name, shape, dt: actx.enter_context(
            nc.sbuf_tensor(prefix + name, shape, dt))
        aps = lambda name, shape, dt: actx.enter_context(
            nc.psum_tensor(prefix + name, shape, dt))
        wkT = asb("_wkT", [128, NDC, D], BF16)
        wvT = asb("_wvT", [128, NDC, D], BF16)
        xnTt = [asb(f"_xnTt2{i}", [128, NDC, 128], BF16) for i in range(2)]
        qkb = [asb(f"_qkb2{i}", [128, D], BF16) for i in range(2)]
        if rope:
            qf = asb("_qf2", [128, D], F32)
            rot = asb("_rot2", [128, D], F32)
            ctile = [asb("_ct2", [128, D], F32)] * 2
            stile = [asb("_st2", [128, D], F32)] * 2
        psP = [aps(f"_psP2{i}", [128, D], F32) for i in range(2)]
        psTR = [aps(f"_psTR2{i}", [128, 1024], BF16) for i in range(2)]

        e_wk = p.step("sync", lambda s: s.dma_start(out=wkT[:], in_=wT_h["k"]),
                      wait=evA1, dma=True)
        e_wv = p.step("sync", lambda s: s.dma_start(out=wvT[:], in_=wT_h["v"]),
                      dma=True)

        ev_xnl = {}
        ev_kev = {}
        ev_ktr = {}
        ev_kevK = {}
        ev_vev = {}
        ev_ropeq = {}
        ev_mmk = {}
        last_ps = [None]
        for t in range(T):
            par = t % 2
            e_xn = p.step("sync",
                          lambda s, t=t, par=par: s.dma_start(
                              out=xnTt[par][:],
                              in_=xnT_h[:, :, t * 128:(t + 1) * 128]),
                          wait=evA1 + [ev_mmk.get(t - 2)], dma=True)
            if rope:
                e_c = p.step("sync",
                             lambda s, t=t, par=par: s.dma_start(
                                 out=ctile[par][:],
                                 in_=io["cos"][t * 128:(t + 1) * 128, :]),
                             wait=evA1 + [ev_ropeq.get(t - 2)], dma=True)
                e_s = p.step("sync",
                             lambda s, t=t, par=par: s.dma_start(
                                 out=stile[par][:],
                                 in_=io["sin"][t * 128:(t + 1) * 128, :]),
                             dma=True)
            else:
                e_c = e_s = None

            def mmk(t_, par=par):
                if biases:
                    for n in range(2):
                        t_.matmul(psP[0][:, n * 512:(n + 1) * 512],
                                  ones1f[:], bk[:, n * 512:(n + 1) * 512],
                                  start=True, stop=False)
                last = None
                for c in range(NDC):
                    for n in range(2):
                        last = t_.matmul(psP[0][:, n * 512:(n + 1) * 512],
                                         xnTt[par][:, c, :],
                                         wkT[:, c, n * 512:(n + 1) * 512],
                                         start=(not biases and c == 0),
                                         stop=(c == NDC - 1))
                return last
            e_mmk = p.step("tensor", mmk,
                           wait=[e_xn, e_wk, e_bias, ev_kev.get(t - 1)])
            if rope:
                def evrk(v, par=par):
                    v.tensor_copy(qf[:], psP[0][:])
                    ro = rot[:].rearrange("p (a two) -> p a two", two=2)
                    qfv = qf[:].rearrange("p (a two) -> p a two", two=2)
                    v.tensor_scalar_mul(ro[:, :, 0:1], qfv[:, :, 1:2], -1.0)
                    v.tensor_copy(ro[:, :, 1:2], qfv[:, :, 0:1])
                    v.tensor_mul(qf[:], qf[:], ctile[par][:])
                    v.tensor_mul(rot[:], rot[:], stile[par][:])
                    v.tensor_add(qkb[par][:], qf[:], rot[:])
                    return v.tensor_copy(jnk[:, 0:1], qkb[par][:, 1023:1024])
                e_ev = p.step("vector", evrk,
                              wait=[e_mmk, e_c, e_s, ev_ktr.get(t - 2)])
                ev_ropeq[t] = e_ev
            else:
                def evk0(v, par=par):
                    v.tensor_copy(qkb[par][:], psP[0][:])
                    return v.tensor_copy(jnk[:, 0:1], qkb[par][:, 1023:1024])
                e_ev = p.step("vector", evk0,
                              wait=[e_mmk, ev_ktr.get(t - 2)])
            ev_kev[t] = e_ev

            def ktr(t_, par=par):
                insts = []
                for c in range(NDC):
                    insts.append(t_.transpose(
                        psTR[par][:, c * 128:(c + 1) * 128],
                        qkb[par][:, c * 128:(c + 1) * 128], idt[:]))
                return insts[-1]
            e_tr = p.step("tensor", ktr, wait=[e_ev, last_ps[0]])
            ev_ktr[t] = e_tr
            def evk(v, par=par, t=t):
                src = psTR[par][:].rearrange("p (c q) -> p c q", q=128)
                v.tensor_copy(KT[:, :, t * 128:(t + 1) * 128], src)
                return v.tensor_copy(jnk[:, 0:NDC],
                                     KT[:, :, t * 128 + 127:t * 128 + 128]
                                     .rearrange("p c q -> p (c q)"))
            e_evk = p.step("vector", evk, wait=[e_tr])
            ev_kevK[t] = e_evk
            last_ps[0] = e_evk

            def mmv(t_, par=par):
                if biases:
                    for n in range(2):
                        t_.matmul(psP[1][:, n * 512:(n + 1) * 512],
                                  ones1f[:], bv[:, n * 512:(n + 1) * 512],
                                  start=True, stop=False)
                last = None
                for c in range(NDC):
                    for n in range(2):
                        last = t_.matmul(psP[1][:, n * 512:(n + 1) * 512],
                                         xnTt[par][:, c, :],
                                         wvT[:, c, n * 512:(n + 1) * 512],
                                         start=(not biases and c == 0),
                                         stop=(c == NDC - 1))
                return last
            e_mmv = p.step("tensor", mmv,
                           wait=[e_xn, e_wv, e_bias, ev_vev.get(t - 1)])
            ev_mmk[t] = e_mmv
            def evv(v, t=t):
                src = psP[1][:].rearrange("p (h c) -> p h c", c=HD)
                dst = Vb[:, t, :].rearrange("p (h c) -> p h c",
                                            c=HD + 2)[:, :, 0:HD]
                v.tensor_copy(dst, src)
                return v.tensor_copy(jnk[:, 0:16],
                                     Vb[:, t, :].rearrange(
                                         "p (h c) -> p h c",
                                         c=HD + 2)[:, :, HD - 1:HD]
                                     .rearrange("p h c -> p (h c)"))
            ev_vev[t] = p.step("vector", evv, wait=[e_mmv, e_vones])
            if debug and t == T - 1:
                dbv = nc.dram_tensor(prefix + "_dbg_bv", [1, D], BF16,
                                     kind="ExternalOutput").ap()
                p.step("sync", lambda s: s.dma_start(out=dbv, in_=bv[:]),
                       wait=[e_mmv], dma=True)
                dpsp = nc.dram_tensor(prefix + "_dbg_psP", [128, D], F32,
                                      kind="ExternalOutput").ap()
                psp_sb = asb("_dbg_psp_sb", [128, D], F32)
                e_dp2 = p.step("vector",
                               lambda v: v.tensor_copy(psp_sb[:], psP[1][:]),
                               wait=[e_mmv])
                p.step("sync",
                       lambda s: s.dma_start(out=dpsp, in_=psp_sb[:]),
                       wait=[e_dp2], dma=True)
        evA_last = [ev_vev[T - 1], ev_kevK[T - 1], last_ps[0]]

    with ExitStack() as bctx:
        bsb = lambda name, shape, dt: bctx.enter_context(
            nc.sbuf_tensor(prefix + name, shape, dt))
        bps = lambda name, shape, dt: bctx.enter_context(
            nc.psum_tensor(prefix + name, shape, dt))
        woT = bsb("_woT", [128, NDC, D], BF16)
        nqb_buf = 1 if causal else 2
        QTs = [bsb("_QTs0", [128, NDC, 512], BF16)] * 2
        QTl = [bsb("_QTl0", [128, NDC, 512], BF16)] * 2
        PT = [bsb(f"_PT{i}", [128, 512], BF16) for i in range(2)]
        cst = [bsb("_cst", [64, 512], F32)] * 2
        rsb = bsb("_rsb", [1, 512], F32)
        xrl = [bsb("_xrl", [128, D], F32)] * 2
        xob = [bsb("_xob", [128, D], F32)] * 2
        cmask = bsb("_cmask", [128, 4, 512], BF16) if causal else None
        nb10 = bsb("_nb10", [128, 1], F32)
        psS = [bps(f"_psS{i}", [128, 512], F32) for i in range(2)]
        psC = [bps(f"_psC{i}", [65, 512], F32) for i in range(2)]
        psR = bps("_psR", [64, 512], F32)
        psO = bps("_psO", [128, D], F32)

        e_nb = p.step("gpsimd", lambda g: g.memset(nb10[:], -10.0))
        e_wo = p.step("sync", lambda s: s.dma_start(out=woT[:], in_=wT_h["o"]),
                      wait=evA_last, dma=True)
        if causal:
            e_cm = p.step("sync",
                          lambda s: s.dma_start(out=cmask[:], in_=io["cmask"]),
                          wait=evA_last, dma=True)
        else:
            e_cm = None

        ev_qt = {}
        ev_store = {}
        ev_res = {}
        ev_ctx = {}
        ev_exp = {}
        ev_pv = {}
        ev_proj = {}
        ev_cst = {}
        ev_psRmm = {}
        ev_rec = {}
        e_pvl_prev = [None]
        for qb in range(NQB):
            pq = qb % 2
            qt_free = ev_proj.get((qb - 1, 3))
            p.step(
                "sync",
                lambda s, qb=qb, pq=pq: s.dma_start(
                    out=QTs[pq][:], in_=qT_h[:, :, qb * 512:(qb + 1) * 512]),
                wait=evA_last + [qt_free], dma=True)
            ev_qt[qb] = p.step(
                "sync",
                lambda s, qb=qb, pq=pq: s.dma_start(
                    out=QTl[pq][:], in_=qTlo_h[:, :, qb * 512:(qb + 1) * 512]),
                wait=[ev_proj.get((qb - 1, 3))], dma=True)
            kcmax = 4 * qb + 4 if causal else NKC
            for h in range(NH):
                ph = h % 2
                hb = (h % 2) * 64
                hc = h // 2
                for kc in range(kcmax):
                    pk = kc % 2

                    def smm(t_, hb=hb, hc=hc, kc=kc, pk=pk, pq=pq):
                        t_.matmul(
                            psS[pk][:],
                            KT[hb:hb + 64, hc, kc * 128:(kc + 1) * 128],
                            QTs[pq][hb:hb + 64, hc, :],
                            start=True, stop=False)
                        return t_.matmul(
                            psS[pk][:],
                            KT[hb:hb + 64, hc, kc * 128:(kc + 1) * 128],
                            QTl[pq][hb:hb + 64, hc, :],
                            start=False, stop=True)
                    sw = [ev_qt[qb], ev_exp.get((qb, h, kc - 2))]
                    if kc < 2:
                        sw.append(ev_cst.get((qb, h - 1)) if h > 0 else
                                  (ev_cst.get((qb - 1, NH - 1)) if qb > 0
                                   else e_wo))
                    e_smm = p.step("tensor", smm, wait=sw)

                    def sexp(sc, pk=pk):
                        return sc.activation(PT[pk][:], psS[pk][:], AF.Exp,
                                             bias=nb10[:], scale=0.125)
                    ptfree = ev_pv.get((qb, h, kc - 2))
                    if kc < 2:
                        ptfree = e_pvl_prev[0]
                    e_exp = p.step("scalar", sexp, wait=[e_smm, ptfree, e_nb])
                    if causal and kc >= 4 * qb:
                        off = kc - 4 * qb
                        def pmask(v, pk=pk, off=off):
                            v.tensor_mul(PT[pk][:], PT[pk][:],
                                         cmask[:, off, :])
                            return v.tensor_copy(jnk[:, 0:1],
                                                 PT[pk][:, 511:512])
                        e_exp = p.step("vector", pmask, wait=[e_exp, e_cm])
                    ev_exp[(qb, h, kc)] = e_exp

                    def pvmm(t_, h=h, kc=kc, pk=pk, ph=ph, kcmax=kcmax):
                        return t_.matmul(
                            psC[ph][:],
                            Vb[:, kc, h * (HD + 2):h * (HD + 2) + HD + 1],
                            PT[pk][:], start=(kc == 0), stop=(kc == kcmax - 1))
                    pvw = [e_exp]
                    if kc == 0:
                        pvw += [ev_cst.get((qb, h - 2)), ev_rec.get((qb, h - 2))]
                        if qb > 0 and h < 2:
                            pvw += [ev_cst.get((qb - 1, NH - 2 + ph)),
                                    ev_rec.get((qb - 1, NH - 2 + ph))]
                    ev_pv[(qb, h, kc)] = p.step("tensor", pvmm, wait=pvw)
                e_pvl = ev_pv[(qb, h, kcmax - 1)]
                e_pvl_prev[0] = e_pvl
                def sums(v, ph=ph):
                    return v.reciprocal(rsb[:], psC[ph][64:65, :])
                e_rec = p.step("vector", sums,
                               wait=[e_pvl, ev_psRmm.get((qb, h - 1)) or
                                     (ev_psRmm.get((qb - 1, NH - 1)) if qb > 0
                                      else None)])
                ev_rec[(qb, h)] = e_rec
                def rmm(t_):
                    return t_.matmul(psR[:], onesf[:, 0:64], rsb[:],
                                     start=True, stop=True)
                e_rmm = p.step("tensor", rmm,
                               wait=[e_rec, e_onesf,
                                     ev_ctx.get((qb, h - 1)) or
                                     (ev_ctx.get((qb - 1, NH - 1)) if qb > 0
                                      else None)])
                ev_psRmm[(qb, h)] = e_rmm
                e_cst = p.step("scalar",
                               lambda sc, ph=ph: sc.activation(
                                   cst[ph][:], psC[ph][0:64, :], AF.Identity),
                               wait=[e_pvl, e_rec,
                                     ev_ctx.get((qb, h - 1)) or
                                     (ev_ctx.get((qb - 1, NH - 1))
                                      if qb > 0 else None)])
                ev_cst[(qb, h)] = e_cst

                def ctxmul(v, ph=ph, hb=hb, hc=hc):
                    v.tensor_tensor(ctxT[hb:hb + 64, hc, :], cst[ph][:],
                                    psR[:], ALU.mult)
                    return v.tensor_copy(jnk[hb:hb + 64, 0:1],
                                         ctxT[hb:hb + 64, hc, 511:512])
                ev_ctx[(qb, h)] = p.step(
                    "vector", ctxmul,
                    wait=[e_cst, e_rmm, ev_proj.get((qb - 1, 3))])

            for j in range(4):
                def pmm(t_, j=j):
                    if biases:
                        for n in range(2):
                            t_.matmul(psO[:, n * 512:(n + 1) * 512],
                                      ones1f[:], bo[:, n * 512:(n + 1) * 512],
                                      start=True, stop=False)
                    last = None
                    for c in range(NDC):
                        for n in range(2):
                            last = t_.matmul(psO[:, n * 512:(n + 1) * 512],
                                             ctxT[:, c, j * 128:(j + 1) * 128],
                                             woT[:, c, n * 512:(n + 1) * 512],
                                             start=(not biases and c == 0),
                                             stop=(c == NDC - 1))
                    return last
                pw = [ev_ctx[(qb, NH - 1)], e_wo, ev_res.get((qb, j - 1))]
                ev_proj[(qb, j)] = p.step("tensor", pmm, wait=pw)
                tt0 = qb * 4 + j
                e_xr = p.step("sync",
                              lambda s, tt0=tt0, j=j: s.dma_start(
                                  out=xrl[j % 2][:],
                                  in_=x[tt0 * 128:(tt0 + 1) * 128, :]),
                              wait=[ev_res.get((qb, j - 1))], dma=True)
                def res(v, j=j):
                    return v.tensor_tensor(xob[j % 2][:], psO[:], xrl[j % 2][:],
                                           ALU.add)
                ev_res[(qb, j)] = p.step("vector", res,
                                         wait=[ev_proj[(qb, j)], e_xr,
                                               ev_store.get((qb, j - 1))])
                ev_store[(qb, j)] = p.step(
                    "sync",
                    lambda s, tt0=tt0, j=j: s.dma_start(
                        out=y[tt0 * 128:(tt0 + 1) * 128, :], in_=xob[j % 2][:]),
                    wait=[ev_res[(qb, j)]], dma=True)
        if debug:
            dpsr = nc.dram_tensor(prefix + "_dbg_psR", [64, 512], F32,
                                  kind="ExternalOutput").ap()
            dcst = nc.dram_tensor(prefix + "_dbg_cst", [64, 512], F32,
                                  kind="ExternalOutput").ap()
            psr_sb = bsb("_dbg_psr_sb", [64, 512], F32)
            e_dbgev = p.step("vector", lambda v: v.tensor_copy(psr_sb[:], psR[:]),
                             wait=[ev_ctx[(NQB - 1, NH - 1)]])
            p.step("sync", lambda s: s.dma_start(out=dpsr, in_=psr_sb[:]),
                   wait=[e_dbgev], dma=True)
            p.step("sync", lambda s: s.dma_start(out=dcst, in_=cst[1][:]),
                   wait=[e_dbgev], dma=True)
            dctx = nc.dram_tensor(prefix + "_dbg_ctxT", [128, NDC, 512], BF16,
                                  kind="ExternalOutput").ap()
            drsb = nc.dram_tensor(prefix + "_dbg_rsb", [1, 512], F32,
                                  kind="ExternalOutput").ap()
            dkt = nc.dram_tensor(prefix + "_dbg_KT", [128, NDC, NTOK], BF16,
                                 kind="ExternalOutput").ap()
            dvb = nc.dram_tensor(prefix + "_dbg_Vb", [128, NTOK // 128,
                                                      NH * (HD + 2)], BF16,
                                 kind="ExternalOutput").ap()
            e_d = p.step("sync", lambda s: s.dma_start(out=dctx, in_=ctxT[:]),
                         wait=[ev_store[(NQB - 1, 3)],
                               ev_ctx[(NQB - 1, NH - 1)]], dma=True)
            e_d = p.step("sync", lambda s: s.dma_start(out=drsb, in_=rsb[:]),
                         dma=True)
            e_d = p.step("sync", lambda s: s.dma_start(out=dkt, in_=KT[:]),
                         dma=True)
            e_d = p.step("sync", lambda s: s.dma_start(out=dvb, in_=Vb[:]),
                         dma=True)
            return e_d
    return ev_store[(NQB - 1, 3)]


def build_attn(NTOK=4096, causal=False, rope=False, biases=False, debug=False):
    nc = bass.Bass("TRN2", target_bir_lowering=False)
    ctx = ExitStack()
    io = {}
    io["x"] = nc.dram_tensor("x", [NTOK, D], F32, kind="ExternalInput").ap()
    io["y"] = nc.dram_tensor("y", [NTOK, D], F32, kind="ExternalOutput").ap()
    for w in "qkvo":
        io["W" + w] = nc.dram_tensor(f"W{w}", [D, D], F32,
                                     kind="ExternalInput").ap()
        if biases:
            io["b" + w] = nc.dram_tensor(f"b{w}", [D], F32,
                                         kind="ExternalInput").ap()
    if rope:
        io["cos"] = nc.dram_tensor("cos", [NTOK, D], F32,
                                   kind="ExternalInput").ap()
        io["sin"] = nc.dram_tensor("sin", [NTOK, D], F32,
                                   kind="ExternalInput").ap()
    if causal:
        io["cmask"] = nc.dram_tensor("cmask", [128, 4, 512], BF16,
                                     kind="ExternalInput").ap()
    io["ident"] = nc.dram_tensor("ident", [128, 128], F32,
                                 kind="ExternalInput").ap()
    p = Prog(nc, ctx)
    ev_last = emit_attn(nc, ctx, p, io, NTOK=NTOK, causal=causal, rope=rope,
                        biases=biases, debug=debug)
    p.wait_only("sync", [ev_last])
    p.build()
    ctx.close()
    return nc


from concourse.bass_utils import run_bass_kernel_spmd

_N_CORES = 8
_S = 4096
_NH, _HD = 16, 64


def _consts():
    ident = np.eye(128, dtype=np.float32)
    inv = 1.0 / 10000.0 ** (np.arange(0, _HD, 2, dtype=np.float64) / _HD)
    pos = np.arange(_S, dtype=np.float64)
    emb = np.repeat(pos[:, None] * inv[None, :], 2, axis=-1)
    cos = np.tile(np.cos(emb), (1, _NH)).astype(np.float32)
    sin = np.tile(np.sin(emb), (1, _NH)).astype(np.float32)
    import ml_dtypes
    cm = np.zeros((128, 4, 512), np.float32)
    for off in range(4):
        for kr in range(128):
            cm[kr, off, off * 128 + kr:] = 1.0
    cmask = cm.astype(ml_dtypes.bfloat16)
    return ident, cos, sin, cmask


def kernel(**inputs):
    f = lambda k: np.ascontiguousarray(np.asarray(inputs[k], dtype=np.float32))
    x = f("L0_attn_x")
    ident, cos, sin, cmask = _consts()

    nc_moe = build_moe(NTOK=_S, GROUP=2048)
    nc_a0 = build_attn(NTOK=_S, causal=True, rope=True, biases=False)
    nc_f1 = build_ffn1(NTOK=_S)
    nc_a1 = build_attn(NTOK=_S, causal=False, rope=False, biases=True)

    state = [np.ascontiguousarray(x[i]) for i in range(_N_CORES)]

    def launch(nc, shared):
        maps = [dict(x=state[i], **shared) for i in range(_N_CORES)]
        res = run_bass_kernel_spmd(nc, maps, core_ids=list(range(_N_CORES)))
        return [np.asarray(r["y"]) for r in res.results]

    moe_w = dict(rw=f("L0_ffn_router_w"), rb=f("L0_ffn_router_b"),
                 w1=f("L0_ffn_weights1"), b1=f("L0_ffn_bias1"),
                 w2=f("L0_ffn_weights2"), b2=f("L0_ffn_bias2"), ident=ident)
    a0_w = dict(Wq=f("L0_attn_Wq"), Wk=f("L0_attn_Wk"), Wv=f("L0_attn_Wv"),
                Wo=f("L0_attn_Wo"), cos=cos, sin=sin, cmask=cmask, ident=ident)
    f1_w = dict(wg=f("L1_ffn_w_gate"), wu=f("L1_ffn_w_up"),
                wd=f("L1_ffn_w_down"), ident=ident)
    a1_w = dict(Wq=f("L1_attn_W_q"), bq=f("L1_attn_b_q"),
                Wk=f("L1_attn_W_k"), bk=f("L1_attn_b_k"),
                Wv=f("L1_attn_W_v"), bv=f("L1_attn_b_v"),
                Wo=f("L1_attn_W_out"), bo=f("L1_attn_b_out"), ident=ident)

    state = launch(nc_moe, moe_w)
    state = launch(nc_a0, a0_w)
    state = launch(nc_moe, moe_w)
    state = launch(nc_f1, f1_w)
    state = launch(nc_a1, a1_w)
    state = launch(nc_f1, f1_w)
    return np.stack(state).astype(np.float32)


# version 06 (final)
# speedup vs baseline: 1.5873x; 1.5873x over previous
import numpy as np
from contextlib import ExitStack
import concourse.bass as bass
import concourse.mybir as mybir

F32 = mybir.dt.float32
BF16 = mybir.dt.bfloat16
AF = mybir.ActivationFunctionType
ALU = mybir.AluOpType
AX = mybir.AxisListType
EPS = 1e-5
D = 1024

ENGINES = ("sync", "gpsimd", "vector", "scalar", "tensor")


class Prog:

    def __init__(self, nc, ctx):
        self.nc = nc
        self.streams = {e: [] for e in ENGINES}
        self.sems = {}
        self.counts = {}
        for e in ENGINES:
            self.sems[e] = ctx.enter_context(nc.semaphore(f"s_{e}"))
            self.sems[e + "_dma"] = ctx.enter_context(nc.semaphore(f"sd_{e}"))
            self.counts[e] = 0
            self.counts[e + "_dma"] = 0

    def step(self, engine, fn, wait=(), inc=True, dma=False):
        key = engine + "_dma" if dma else engine
        stepn = 16 if dma else 1
        ev = None
        if inc:
            self.counts[key] += stepn
            ev = (self.sems[key], self.counts[key], stepn)
        waits = [w for w in wait if w is not None]

        def run(eng, ev=ev, waits=waits, fn=fn):
            for w in waits:
                eng.wait_ge(w[0], w[1])
            inst = fn(eng)
            if ev is not None:
                if inst is None:
                    raise RuntimeError("step with inc=True returned no inst")
                inst.then_inc(ev[0], ev[2])

        self.streams[engine].append(run)
        return ev

    def barrier(self):
        dma_evs = [(self.sems[e + "_dma"], self.counts[e + "_dma"], 16)
                   for e in ENGINES if self.counts[e + "_dma"] > 0]
        evs = [self.step(e, lambda eng: eng.nop(), wait=dma_evs)
               for e in ENGINES]
        for e in ENGINES:
            self.wait_only(e, evs)
        return evs

    def wait_only(self, engine, wait):
        def run(eng, waits=[w for w in wait if w is not None]):
            for w in waits:
                eng.wait_ge(w[0], w[1])
        self.streams[engine].append(run)

    def build(self):
        with self.nc.Block() as blk:
            for name, dec in (("sync", blk.sync), ("gpsimd", blk.gpsimd),
                              ("vector", blk.vector), ("scalar", blk.scalar),
                              ("tensor", blk.tensor)):
                stream = self.streams[name]
                if not stream:
                    continue

                def section(eng, stream=stream):
                    for fn in stream:
                        fn(eng)
                dec(section)


def emit_ln(p, xt_ap, bufs, evs_in, xn_out_ap):
    dd = xt_ap.shape[-1]
    e_sq = p.step("scalar", lambda sc: sc.activation(bufs["sq"], xt_ap, AF.Square),
                  wait=list(evs_in) + [bufs.get("sq_free")])

    def reds(v):
        v.reduce_sum(bufs["sum"], xt_ap, axis=AX.X)
        return v.reduce_sum(bufs["sumsq"], bufs["sq"], axis=AX.X)
    e_red = p.step("vector", reds, wait=list(evs_in) + [e_sq])

    def stats(g):
        g.tensor_scalar_mul(bufs["mean"], bufs["sum"], 1.0 / dd)
        g.tensor_mul(bufs["msq"], bufs["mean"], bufs["mean"])
        g.tensor_scalar_add(bufs["msq"], bufs["msq"], -EPS)
        g.tensor_scalar_mul(bufs["vpe"], bufs["sumsq"], 1.0 / dd)
        return g.tensor_sub(bufs["vpe"], bufs["vpe"], bufs["msq"])
    e_st = p.step("gpsimd", stats, wait=[e_red])

    def rstd(sc):
        sc.activation(bufs["lnv"], bufs["vpe"], AF.Ln)
        return sc.activation(bufs["rstd"], bufs["lnv"], AF.Exp, scale=-0.5)
    e_rs = p.step("scalar", rstd, wait=[e_st])

    def nmr(g):
        g.tensor_mul(bufs["nmr"], bufs["mean"], bufs["rstd"])
        return g.tensor_scalar_mul(bufs["nmr"], bufs["nmr"], -1.0)
    e_nm = p.step("gpsimd", nmr, wait=[e_rs])

    e_xn = p.step("scalar",
                  lambda sc: sc.activation(xn_out_ap, xt_ap, AF.Identity,
                                           bias=bufs["nmr"], scale=bufs["rstd"]),
                  wait=[e_nm, bufs.get("xn_free")])
    return e_xn, e_red


def emit_moe(nc, ctx, p, io, NTOK=4096, NEXP=32, HEXP=256, GROUP=2048,
             prefix="m"):
    NG = NTOK // GROUP
    T = GROUP // 128
    NDC = D // 128
    NHC = HEXP // 128
    NE = NEXP
    x, y = io["x"], io["y"]
    rw, rb, w1, b1, w2, b2 = io["rw"], io["rb"], io["w1"], io["b1"], io["w2"], io["b2"]
    ident = io["ident"]

    sbctx = ExitStack()
    sb = lambda name, shape, dt: sbctx.enter_context(
        nc.sbuf_tensor(prefix + name, shape, dt))
    ps_ = lambda name, shape, dt: sbctx.enter_context(
        nc.psum_tensor(prefix + name, shape, dt))

    xacc = sb("_xacc", [128, T, D], F32)
    xnbuf = sb("_xn", [128, 2, D], BF16)
    sqbuf = sb("_sq", [128, 2, D], F32)
    xnT = sb("_xnT", [128, NDC, GROUP], BF16)
    rw_st = sb("_rwst", [NEXP, D], F32)
    rwT = sb("_rwT", [128, NDC * NEXP], F32)
    rbT = sb("_rbT", [1, NEXP], F32)
    idt32 = sb("_idt32", [128, 128], F32)
    idt = sb("_idt", [128, 128], BF16)
    ones1 = sb("_ones1", [1, 128], BF16)
    ones1fb = sb("_ones1fb", [1, 128], F32)
    zrow = sb("_zrow", [1, 128], BF16)
    zrowf = sb("_zrowf", [1, 128], F32)
    ones1f = sb("_ones1f", [1, 128], F32)
    xnf = sb("_xnf", [128, 2, D], F32)
    xnT32 = [sb(f"_xnT32{i}", [128, D], F32) for i in range(2)]
    st = sb("_stats", [128, 8, T], F32)
    lgt = sb("_lgt", [128, T, NEXP], F32)
    mx = sb("_mx", [128, T], F32)
    mx2 = sb("_mx2", [128, T], F32)
    hm = sb("_hm", [128, T, NEXP], F32)
    halfc = sb("_halfc", [128, NEXP], F32)
    jnk = sb("_jnk", [128, 8], F32)
    w1sb = [sb(f"_w1sb{i}", [128, NDC, HEXP], BF16) for i in range(2)]
    w2sb = [sb(f"_w2sb{i}", [128, NHC, D], BF16) for i in range(2)]
    b1sb = [sb(f"_b1sb{i}", [1, HEXP], F32) for i in range(2)]
    b2sb = [sb(f"_b2sb{i}", [1, D], F32) for i in range(2)]
    hsb = [sb(f"_hsb{i}", [128, HEXP], BF16) for i in range(2)]
    hT = [sb(f"_hT{i}", [128, NHC, 128], BF16) for i in range(2)]
    ytmp = [sb(f"_ytmp{i}", [128, D], F32) for i in range(2)]

    psA = [ps_(f"_psA{i}", [128, 512], F32) for i in range(2)]
    psTR = [ps_(f"_psTR{i}", [128, 1024], BF16) for i in range(2)]
    psY = [ps_(f"_psY{i}", [128, D], F32) for i in range(2)]

    def stb(row, t):
        return st[:, row, t:t + 1]

    e_id32 = p.step("sync", lambda s: s.dma_start(out=idt32[:], in_=ident),
                    dma=True)
    e_id = p.step("vector", lambda v: v.tensor_copy(idt[:], idt32[:]),
                  wait=[e_id32])
    e_rw = p.step("sync", lambda s: s.dma_start(out=rw_st[:], in_=rw), dma=True)
    e_rb = p.step("sync",
                  lambda s: s.dma_start(out=rbT[:], in_=rb.unsqueeze(0)),
                  dma=True)
    p.step("gpsimd", lambda g: g.memset(halfc[:], 0.5), inc=False)
    p.step("gpsimd", lambda g: g.memset(ones1[:], 1.0), inc=False)
    p.step("gpsimd", lambda g: g.memset(ones1fb[:], 1.0), inc=False)
    p.step("gpsimd", lambda g: g.memset(zrow[:], 0.0), inc=False)
    p.step("gpsimd", lambda g: g.memset(zrowf[:], 0.0), inc=False)
    e_ones = p.step("gpsimd", lambda g: g.memset(ones1f[:], 1.0))

    def rtr(t_):
        insts = []
        for c in range(NDC):
            insts.append(t_.transpose(psY[0][:, c * NEXP:(c + 1) * NEXP],
                                      rw_st[:, c * 128:(c + 1) * 128],
                                      idt32[0:NEXP, 0:NEXP]))
        return insts[-1]
    e_rtr = p.step("tensor", rtr, wait=[e_id32, e_rw])
    def evrwT(v):
        v.tensor_copy(rwT[:], psY[0][:, 0:NDC * NEXP])
        return v.tensor_copy(jnk[:, 0:1], rwT[:, NDC * NEXP - 1:NDC * NEXP])
    e_rwT = p.step("vector", evrwT, wait=[e_rtr])

    ev_store_prev = None
    ev_ymm = {}
    ev_reluh = {}
    ev_reluy = {}
    ev_vhT = {}
    ev_vacc = {}
    ev_red = {}
    ev_ptr = {}
    ev_vlg = {}
    ev_vhm = {}
    ev_evxnT = {}
    ev_ptr32 = {}
    ev_evx32 = {}

    for g in range(NG):
        xg = x.rearrange("(gg t p) d -> gg p t d", p=128, t=T)[g]
        yg = y.rearrange("(gg t p) d -> gg p t d", p=128, t=T)[g]
        e_x = p.step("sync", lambda s, xg=xg: s.dma_start(out=xacc[:], in_=xg),
                     wait=[ev_store_prev], dma=True)

        for t in range(T):
            par = t % 2
            bufs = dict(sq=sqbuf[:, par, :],
                        sum=stb(0, t), sumsq=stb(1, t), mean=stb(2, t),
                        msq=stb(3, t), vpe=stb(4, t), rstd=stb(5, t),
                        nmr=stb(6, t), lnv=stb(7, t),
                        sq_free=ev_red.get((g, t - 2)),
                        xn_free=ev_ptr.get((g, t - 2)))
            bufs["xn_free"] = ev_ptr32.get((g, t - 2))
            exnf, ered = emit_ln(p, xacc[:, t, :], bufs, [e_x], xnf[:, par, :])
            ev_red[(g, t)] = ered
            exn = p.step("scalar",
                         lambda sc, par=par: sc.activation(
                             xnbuf[:, par, :], xnf[:, par, :], AF.Identity),
                         wait=[exnf, ev_ptr.get((g, t - 2))])

            def tr32(t_, par=par):
                insts = []
                for c in range(NDC):
                    insts.append(t_.transpose(
                        psY[par][:, c * 128:(c + 1) * 128],
                        xnf[:, par, c * 128:(c + 1) * 128], idt32[:]))
                return insts[-1]
            t32_wait = [exnf, ev_evx32.get((g, t - 2))]
            if g == 0 and t < 2:
                t32_wait.append(e_rwT if t == 0 else None)
            if g > 0 and t < 2:
                t32_wait.append(ev_reluy.get((g - 1, NE - 1, T - 2 + par)))
            ev_ptr32[(g, t)] = p.step("tensor", tr32, wait=t32_wait)
            def evx32(v, par=par):
                v.tensor_copy(xnT32[par][:], psY[par][:])
                return v.tensor_copy(jnk[:, 0:1], xnT32[par][:, 1023:1024])
            ev_evx32[(g, t)] = p.step("vector", evx32,
                                      wait=[ev_ptr32[(g, t)]])

            def tr(t_, par=par):
                insts = []
                for c in range(NDC):
                    insts.append(t_.transpose(
                        psTR[par][:, c * 128:(c + 1) * 128],
                        xnbuf[:, par, c * 128:(c + 1) * 128], idt[:]))
                return insts[-1]
            tr_wait = [exn, ev_evxnT.get((g, t - 2))]
            if t < 2 and g > 0:
                tr_wait.append(ev_vhT.get((g - 1, NE - 1, T - 2 + par)))
            ev_ptr[(g, t)] = p.step("tensor", tr, wait=tr_wait)

            def evx(v, par=par, t=t):
                src = psTR[par][:, 0:1024].rearrange("p (c q) -> p c q", q=128)
                v.tensor_copy(xnT[:, :, t * 128:(t + 1) * 128], src)
                return v.tensor_copy(jnk[:, 0:8],
                                     xnT[:, :, t * 128 + 127].squeeze(2)
                                     if False else
                                     xnT[:, :, t * 128 + 127:t * 128 + 128]
                                     .rearrange("p c q -> p (c q)"))
            evw = [ev_ptr[(g, t)]]
            if g > 0 and t == 0:
                evw.append(ev_ymm[(g - 1, NE - 1)])
            ev_evxnT[(g, t)] = p.step("vector", evx, wait=evw)

            def rt(t_, par=par, t=t):
                t_.matmul(psA[par][:, 0:NEXP], ones1f[:], rbT[:],
                          start=True, stop=False)
                for c in range(NDC):
                    t_.matmul(psA[par][:, 0:NEXP],
                              xnT32[par][:, c * 128:(c + 1) * 128],
                              rwT[:, c * NEXP:(c + 1) * NEXP],
                              start=False, stop=False)
                return t_.matmul(psA[par][:, 0:NEXP], zrowf[:], rbT[:],
                                 start=False, stop=True)
            rt_wait = [ev_evx32[(g, t)], e_ones, e_rb, ev_vlg.get((g, t - 2))]
            if t < 2 and g > 0:
                rt_wait.append(ev_reluh.get((g - 1, NE - 1, T - 2 + par)))
            e_prt = p.step("tensor", rt, wait=rt_wait)

            def lg_(v, par=par, t=t):
                v.tensor_copy(lgt[:, t, :], psA[par][:, 0:NEXP])
                return v.reduce_max(mx[:, t:t + 1], lgt[:, t, :], axis=AX.X)
            ev_vlg[(g, t)] = p.step("vector", lg_, wait=[e_prt])

            e_mx2 = p.step("gpsimd",
                           lambda g_, t=t: g_.tensor_copy(mx2[:, t:t + 1],
                                                          mx[:, t:t + 1]),
                           wait=[ev_vlg[(g, t)]])

            ev_vhm[(g, t)] = p.step(
                "vector",
                lambda v, t=t: v.scalar_tensor_tensor(
                    hm[:, t, :], lgt[:, t, :], mx2[:, t:t + 1], halfc[:],
                    ALU.is_equal, ALU.mult),
                wait=[e_mx2])

        for e in range(NE):
            wp = e % 2
            if e >= 2:
                wfree = ev_ymm.get((g, e - 2))
            elif g > 0:
                wfree = ev_ymm.get((g - 1, NE - 2 + wp))
            else:
                wfree = None

            p.step("gpsimd",
                   lambda g_, e=e, wp=wp: g_.dma_start(
                       out=w1sb[wp][:],
                       in_=w1[e].rearrange("(c p) h -> p c h", p=128)),
                   wait=[wfree], dma=True)
            e_wl = p.step("gpsimd",
                          lambda g_, e=e, wp=wp: g_.dma_start(
                              out=w2sb[wp][:],
                              in_=w2[e].rearrange("(c p) d -> p c d", p=128)),
                          dma=True)
            p.step("sync",
                   lambda s_, e=e, wp=wp: s_.dma_start(out=b1sb[wp][:],
                                                       in_=b1[e:e + 1, :]),
                   dma=True)
            e_wl2 = p.step("sync",
                           lambda s_, e=e, wp=wp: s_.dma_start(
                               out=b2sb[wp][:], in_=b2[e:e + 1, :]),
                           dma=True)

            for t in range(T):
                par = t % 2

                def hmm(t_, e=e, t=t, par=par, wp=wp):
                    t_.matmul(psA[par][:, 0:HEXP], ones1f[:], b1sb[wp][:],
                              start=True, stop=False)
                    for c in range(NDC):
                        t_.matmul(psA[par][:, 0:HEXP],
                                  xnT[:, c, t * 128:(t + 1) * 128],
                                  w1sb[wp][:, c, :], start=False, stop=False)
                    return t_.matmul(psA[par][:, 0:HEXP], zrowf[:],
                                     b1sb[wp][:], start=False, stop=True)
                if t >= 2:
                    pafree = ev_reluh.get((g, e, t - 2))
                elif e >= 1:
                    pafree = ev_reluh.get((g, e - 1, T - 2 + par))
                else:
                    pafree = ev_vlg.get((g, T - 2 + par))
                hw = [e_wl, e_wl2, pafree]
                if e == 0 and t == 0:
                    hw.append(ev_vhm[(g, T - 1)])
                e_hmm = p.step("tensor", hmm, wait=hw)

                if t >= 2:
                    hsfree = ev_vhT.get((g, e, t - 2))
                elif e >= 1:
                    hsfree = ev_vhT.get((g, e - 1, T - 2 + par))
                elif g > 0:
                    hsfree = ev_vhT.get((g - 1, NE - 1, T - 2 + par))
                else:
                    hsfree = None
                ev_reluh[(g, e, t)] = p.step(
                    "scalar",
                    lambda sc, par=par: sc.activation(hsb[par][:],
                                                      psA[par][:, 0:HEXP],
                                                      AF.Relu),
                    wait=[e_hmm, hsfree])

                def htr(t_, par=par):
                    insts = []
                    for c in range(NHC):
                        insts.append(t_.transpose(
                            psTR[par][:, c * 128:(c + 1) * 128],
                            hsb[par][:, c * 128:(c + 1) * 128], idt[:]))
                    return insts[-1]
                if t >= 2:
                    ptrfree = ev_vhT.get((g, e, t - 2))
                elif e >= 1:
                    ptrfree = ev_vhT.get((g, e - 1, T - 2 + par))
                else:
                    ptrfree = ev_evxnT.get((g, T - 2 + par))
                e_phtr = p.step("tensor", htr,
                                wait=[ev_reluh[(g, e, t)], ptrfree])

                def evh(v, par=par):
                    src = psTR[par][:, 0:HEXP].rearrange("p (c q) -> p c q",
                                                         q=128)
                    v.tensor_copy(hT[par][:], src)
                    return v.tensor_copy(jnk[:, 0:NHC],
                                         hT[par][:, :, 127:128]
                                         .rearrange("p c q -> p (c q)"))
                ev_vhT[(g, e, t)] = p.step("vector", evh, wait=[e_phtr])

                def ymm(t_, e=e, t=t, par=par, wp=wp):
                    last = None
                    for n in range(2):
                        t_.matmul(psY[par][:, n * 512:(n + 1) * 512],
                                  ones1f[:],
                                  b2sb[wp][:, n * 512:(n + 1) * 512],
                                  start=True, stop=False)
                        for c in range(NHC):
                            last = t_.matmul(psY[par][:, n * 512:(n + 1) * 512],
                                             hT[par][:, c, :],
                                             w2sb[wp][:, c, n * 512:(n + 1) * 512],
                                             start=False, stop=(c == NHC - 1))
                    return last
                if t >= 2:
                    pyfree = ev_reluy.get((g, e, t - 2))
                elif e >= 1:
                    pyfree = ev_reluy.get((g, e - 1, T - 2 + par))
                else:
                    pyfree = ev_evx32.get((g, T - 2 + par))
                ev_ymm[(g, e)] = e_ymm = p.step("tensor", ymm,
                                                wait=[ev_vhT[(g, e, t)], pyfree])

                if t >= 2:
                    ytfree = ev_vacc.get((g, e, t - 2))
                elif e >= 1:
                    ytfree = ev_vacc.get((g, e - 1, T - 2 + par))
                elif g > 0:
                    ytfree = ev_vacc.get((g - 1, NE - 1, T - 2 + par))
                else:
                    ytfree = None
                ev_reluy[(g, e, t)] = p.step(
                    "scalar",
                    lambda sc, par=par: sc.activation(ytmp[par][:], psY[par][:],
                                                      AF.Relu),
                    wait=[e_ymm, ytfree])

                ev_vacc[(g, e, t)] = p.step(
                    "vector",
                    lambda v, e=e, t=t, par=par: v.scalar_tensor_tensor(
                        xacc[:, t, :], ytmp[par][:], hm[:, t, e:e + 1],
                        xacc[:, t, :], ALU.mult, ALU.add),
                    wait=[ev_reluy[(g, e, t)]])

        ev_store_prev = p.step(
            "sync", lambda s, yg=yg: s.dma_start(out=yg, in_=xacc[:]),
            wait=[ev_vacc[(g, NE - 1, T - 1)]], dma=True)

    sbctx.close()
    return ev_store_prev


def build_moe(NTOK=4096, NEXP=32, HEXP=256, GROUP=2048):
    nc = bass.Bass("TRN2", target_bir_lowering=False)
    ctx = ExitStack()
    io = {}
    io["x"] = nc.dram_tensor("x", [NTOK, D], F32, kind="ExternalInput").ap()
    io["y"] = nc.dram_tensor("y", [NTOK, D], F32, kind="ExternalOutput").ap()
    io["rw"] = nc.dram_tensor("rw", [NEXP, D], F32, kind="ExternalInput").ap()
    io["rb"] = nc.dram_tensor("rb", [NEXP], F32, kind="ExternalInput").ap()
    io["w1"] = nc.dram_tensor("w1", [NEXP, D, HEXP], F32, kind="ExternalInput").ap()
    io["b1"] = nc.dram_tensor("b1", [NEXP, HEXP], F32, kind="ExternalInput").ap()
    io["w2"] = nc.dram_tensor("w2", [NEXP, HEXP, D], F32, kind="ExternalInput").ap()
    io["b2"] = nc.dram_tensor("b2", [NEXP, D], F32, kind="ExternalInput").ap()
    io["ident"] = nc.dram_tensor("ident", [128, 128], F32,
                                 kind="ExternalInput").ap()
    p = Prog(nc, ctx)
    ev_last = emit_moe(nc, ctx, p, io, NTOK=NTOK, NEXP=NEXP, HEXP=HEXP,
                       GROUP=GROUP)
    p.wait_only("sync", [ev_last])
    p.build()
    ctx.close()
    return nc


def make_ident():
    import ml_dtypes
    return np.eye(128, dtype=ml_dtypes.bfloat16)


I32 = mybir.dt.int32


def emit_quant(p, out_ap, in_ap, qa, evs_in, out_free=None):
    n = in_ap.shape[-1]
    nb = n // 16
    bm = qa["bm"][:, 0:nb]
    sc = qa["sc"][:, 0:nb]
    rsc = qa["rsc"][:, 0:nb]
    rsc2 = qa["rsc2"][:, 0:nb]
    v = qa["v"][:, 0:n]
    u = qa["u"][:, 0:n]
    acc = qa["acc"][:, 0:n]
    bb = qa["bb"][:, 0:n]
    ib = qa["ib"][:, 0:n]
    sg = qa["sg"][:, 0:n]
    in3 = in_ap.rearrange("p (b k) -> p b k", k=16)

    e_bm = p.step("vector",
                  lambda ve: ve.tensor_reduce(bm, in3, axis=AX.X, op=ALU.max,
                                              apply_absolute_value=True),
                  wait=list(evs_in) + [qa.get("free")])
    e_sc = p.step("gpsimd",
                  lambda g: g.tensor_scalar(sc, bm, 1.0 / 6.0, 1e-12,
                                            ALU.mult, ALU.max),
                  wait=[e_bm, qa.get("prev_out")])
    e_rsc = p.step("vector", lambda ve: ve.reciprocal(rsc, sc), wait=[e_sc])
    e_rsc2 = p.step("gpsimd", lambda g: g.tensor_copy(rsc2, rsc), wait=[e_rsc])

    rb = rsc2.unsqueeze(2).broadcast_to([128, nb, 16])
    v3 = v.rearrange("p (b k) -> p b k", k=16)
    e_v = p.step("vector",
                 lambda ve: ve.tensor_tensor(v3, in3, rb, ALU.mult),
                 wait=[e_rsc2])
    e_sg = p.step("scalar", lambda s: s.activation(sg, v, AF.Sign), wait=[e_v])

    def core(ve):
        ve.tensor_mul(u, v, sg)
        ve.tensor_scalar_mul(bb, u, 2.0)
        ve.tensor_copy(ib, bb)
        ve.tensor_scalar(acc, ib, 0.5, 2.0, ALU.mult, ALU.min)
        ve.tensor_copy(ib, u)
        ve.tensor_scalar(bb, ib, 2.0, 4.0, ALU.max, ALU.min)
        ve.tensor_add(acc, acc, bb)
        ve.tensor_scalar_mul(bb, u, 0.5)
        ve.tensor_copy(ib, bb)
        ve.tensor_scalar(bb, ib, 2.0, 4.0, ALU.mult, ALU.max)
        ve.tensor_add(acc, acc, bb)
        return ve.tensor_scalar_add(acc, acc, -6.0)
    e_core = p.step("vector", core, wait=[e_v, e_sg])

    e_m = p.step("vector", lambda ve: ve.tensor_mul(acc, acc, sg),
                 wait=[e_core, e_sg])
    scb = sc.unsqueeze(2).broadcast_to([128, nb, 16])
    acc3 = acc.rearrange("p (b k) -> p b k", k=16)
    out3 = out_ap.rearrange("p (b k) -> p b k", k=16)
    def qout(ve):
        ve.tensor_tensor(out3, acc3, scb, ALU.mult)
        return ve.tensor_copy(qa["rsc"][:, 0:1], out_ap[:, n - 1:n])
    e_out = p.step("vector", qout,
                   wait=[e_m] + ([out_free] if out_free else []))
    qa["prev_out"] = e_out
    return e_out


def emit_ffn1(nc, ctx, p, io, NTOK=4096, I=2816, prefix="f"):
    T = NTOK // 128
    NDC = D // 128
    NIC = I // 128
    ISL = 512
    NSL = (I + ISL - 1) // ISL
    x, y = io["x"], io["y"]
    wg, wu, wd, ident = io["wg"], io["wu"], io["wd"], io["ident"]

    sbctx = ExitStack()
    sb = lambda name, shape, dt: sbctx.enter_context(
        nc.sbuf_tensor(prefix + name, shape, dt))
    ps_ = lambda name, shape, dt: sbctx.enter_context(
        nc.psum_tensor(prefix + name, shape, dt))

    idt32 = sb("_idt32", [128, 128], F32)
    idt = sb("_idt", [128, 128], BF16)
    wgT = sb("_wgT", [128, NDC, I], BF16)
    wuT = sb("_wuT", [128, NDC, I], BF16)
    wdT = sb("_wdT", [128, NIC, D], BF16)
    wraw = [sb("_wraw", [128, 2816], F32)] * 2
    wq = [sb("_wq", [128, 2816], BF16)] * 2
    qa = dict(bm=sb("_qbm", [128, 32], F32), sc=sb("_qsc", [128, 32], F32),
              rsc=sb("_qrsc", [128, 32], F32), rsc2=sb("_qrsc2", [128, 32], F32),
              v=sb("_qv", [128, 512], F32), u=sb("_qu", [128, 512], F32),
              acc=sb("_qacc", [128, 512], F32), bb=sb("_qbb", [128, 512], F32),
              ib=sb("_qib", [128, 512], I32), sg=sb("_qsg", [128, 512], F32))
    xt = [sb(f"_xt{i}", [128, D], F32) for i in range(2)]
    xnf = [sb(f"_xnf{i}", [128, D], F32) for i in range(2)]
    xq = [sb(f"_xq{i}", [128, D], BF16) for i in range(2)]
    xqT = [sb(f"_xqT{i}", [128, NDC, 128], BF16) for i in range(2)]
    hs = sb("_hs", [128, ISL], F32)
    hh = sb("_hh", [128, ISL], F32)
    hq = sb("_hq", [128, ISL], BF16)
    hqT = sb("_hqT", [128, 4, 128], BF16)
    xout = [sb(f"_xout{i}", [128, D], F32) for i in range(2)]
    jnk = sb("_jnk", [128, 32], F32)
    sqln0 = sb("_sqln", [128, D], F32)
    sqln = sqln0.rearrange("p (g d) -> p g d", g=1).broadcast_to([128, 1, D]) if False else None
    st = sb("_stats", [128, 8, 2], F32)

    psG = ps_("_psG", [128, ISL], F32)
    psU = ps_("_psU", [128, ISL], F32)
    psD = ps_("_psD", [128, D], F32)
    psTR = ps_("_psTR", [128, 1024], BF16)

    def stb(row, par):
        return st[:, row, par:par + 1]

    e_id32 = p.step("sync", lambda s: s.dma_start(out=idt32[:], in_=ident),
                    dma=True)
    e_id = p.step("vector", lambda v: v.tensor_copy(idt[:], idt32[:]),
                  wait=[e_id32])

    ev_wtr_prev = {}
    ev_evw_prev = None
    e_q_prev = None
    chunk_jobs = []
    for wi, (wsrc, wdstT) in enumerate(((wg, wgT), (wu, wuT))):
        for c in range(NIC):
            chunk_jobs.append(("gu", wi, c, wsrc, wdstT))
    for c in range(NDC):
        chunk_jobs.append(("dn", 2, c, wd, wdT))

    ev_wload = {}
    ev_quant = {}
    ev_wtr = {}
    ev_evw = {}
    last_ps_reader = [None]
    for j, (kind, wi, c, wsrc, wdstT) in enumerate(chunk_jobs):
        par = j % 2
        ncols = 1024 if kind == "gu" else 2816
        ev_wload[j] = p.step(
            "sync",
            lambda s, wsrc=wsrc, c=c, par=par, ncols=ncols: s.dma_start(
                out=wraw[par][:, 0:ncols], in_=wsrc[c * 128:(c + 1) * 128, :]),
            wait=[ev_quant.get(j - 1)], dma=True)
        e_q = None
        for s0 in range(0, ncols, 512):
            s1 = min(s0 + 512, ncols)
            e_q = emit_quant(p, wq[par][:, s0:s1], wraw[par][:, s0:s1], qa,
                             [ev_wload[j], e_q_prev],
                             out_free=ev_wtr.get(j - 1))
            e_q_prev = e_q
        ev_quant[j] = e_q

        nblk = ncols // 128
        def wtr(t_, par=par, nblk=nblk):
            insts = []
            for b in range(nblk):
                insts.append(t_.transpose(
                    psTR[:, (b % 8) * 128:((b % 8) + 1) * 128],
                    wq[par][:, b * 128:(b + 1) * 128], idt[:]))
            return insts[-1]
        last_evw = None
        for b0 in range(0, nblk, 8):
            b1 = min(b0 + 8, nblk)
            def wtr8(t_, par=par, b0=b0, b1=b1):
                insts = []
                for b in range(b0, b1):
                    insts.append(t_.transpose(
                        psTR[:, (b - b0) * 128:(b - b0 + 1) * 128],
                        wq[par][:, b * 128:(b + 1) * 128], idt[:]))
                return insts[-1]
            e_tr = p.step("tensor", wtr8,
                          wait=[ev_quant[j], e_id, last_ps_reader[0]])
            def evw(v, kind=kind, wi=wi, c=c, b0=b0, b1=b1, wdstT=wdstT):
                for b in range(b0, b1):
                    src = psTR[:, (b - b0) * 128:(b - b0 + 1) * 128]
                    dst = wdstT[:, b, c * 128:(c + 1) * 128]
                    v.tensor_copy(dst, src)
                return v.tensor_copy(
                    jnk[:, 0:b1 - b0],
                    wdstT[:, b0:b1, c * 128 + 127:c * 128 + 128]
                    .rearrange("p c q -> p (c q)"))
            last_evw = p.step("vector", evw, wait=[e_tr])
            last_ps_reader[0] = last_evw
        ev_wtr[j] = e_tr
        ev_evw[j] = last_evw
    e_wprep = ev_evw[len(chunk_jobs) - 1]

    ev_ln = {}
    ev_lnred = {}
    ev_xq = {}
    ev_xqT = {}
    ev_res = {}
    ev_store = {}
    ev_evxq = {}
    ev_down_done = {}
    e_hqT_ev = None
    e_down_last = None
    e_silu_prev = None
    e_h_prev = None
    e_hq_prev = None
    for t in range(T):
        par = t % 2
        xg = x[t * 128:(t + 1) * 128, :]
        yg = y[t * 128:(t + 1) * 128, :]
        e_x = p.step("sync",
                     lambda s, xg=xg, par=par: s.dma_start(out=xt[par][:], in_=xg),
                     wait=[ev_store.get(t - 2)], dma=True)
        bufs = dict(sq=sqln0[:], sum=stb(0, par), sumsq=stb(1, par),
                    mean=stb(2, par), msq=stb(3, par), vpe=stb(4, par),
                    rstd=stb(5, par), nmr=stb(6, par), lnv=stb(7, par),
                    sq_free=ev_lnred.get(t - 1), xn_free=ev_xq.get(t - 2))
        exn, ered = emit_ln(p, xt[par][:], bufs, [e_x], xnf[par][:])
        ev_ln[t] = exn
        ev_lnred[t] = ered
        e_xq = emit_quant(p, xq[par][:, 0:512], xnf[par][:, 0:512], qa, [exn],
                          out_free=ev_xqT.get(t - 2))
        e_xq = emit_quant(p, xq[par][:, 512:1024], xnf[par][:, 512:1024], qa,
                          [exn], out_free=ev_xqT.get(t - 2))
        ev_xq[t] = e_xq

        def xtr(t_, par=par):
            insts = []
            for c in range(NDC):
                insts.append(t_.transpose(psTR[:, c * 128:(c + 1) * 128],
                                          xq[par][:, c * 128:(c + 1) * 128],
                                          idt[:]))
            return insts[-1]
        e_xtr = p.step("tensor", xtr, wait=[e_xq, last_ps_reader[0]])
        def evxq(v, par=par):
            src = psTR[:, 0:1024].rearrange("p (c q) -> p c q", q=128)
            v.tensor_copy(xqT[par][:], src)
            return v.tensor_copy(jnk[:, 0:NDC],
                                 xqT[par][:, :, 127:128]
                                 .rearrange("p c q -> p (c q)"))
        ev_evxq[t] = p.step("vector", evxq, wait=[e_xtr])
        last_ps_reader[0] = ev_evxq[t]
        ev_xqT[t] = e_xtr

        for si in range(NSL):
            s0 = si * ISL
            s1 = min(s0 + ISL, I)
            w_ = s1 - s0
            def gmm(t_, par=par, s0=s0, w_=w_):
                last = None
                for c in range(NDC):
                    nn = 0
                    while nn < w_:
                        ns = min(512, w_ - nn)
                        last = t_.matmul(psG[:, nn:nn + ns],
                                         xqT[par][:, c, :],
                                         wgT[:, c, s0 + nn:s0 + nn + ns],
                                         start=(c == 0), stop=(c == NDC - 1))
                        nn += ns
                return last
            def umm(t_, par=par, s0=s0, w_=w_):
                last = None
                for c in range(NDC):
                    nn = 0
                    while nn < w_:
                        ns = min(512, w_ - nn)
                        last = t_.matmul(psU[:, nn:nn + ns],
                                         xqT[par][:, c, :],
                                         wuT[:, c, s0 + nn:s0 + nn + ns],
                                         start=(c == 0), stop=(c == NDC - 1))
                        nn += ns
                return last
            e_gmm = p.step("tensor", gmm, wait=[ev_evxq[t], e_silu_prev])
            e_umm = p.step("tensor", umm, wait=[e_h_prev])
            e_silu = p.step("scalar",
                            lambda sc, w_=w_: sc.activation(hs[:, 0:w_],
                                                            psG[:, 0:w_],
                                                            AF.Silu),
                            wait=[e_gmm, e_h_prev])
            e_h = p.step("vector",
                         lambda ve, w_=w_: ve.tensor_tensor(
                             hh[:, 0:w_], hs[:, 0:w_], psU[:, 0:w_], ALU.mult),
                         wait=[e_silu, e_umm, e_hq_prev])
            e_silu_prev = e_silu
            e_h_prev = e_h
            e_hq = emit_quant(p, hq[:, 0:w_], hh[:, 0:w_], qa, [e_h],
                              out_free=e_hqT_ev)
            e_hq_prev = e_hq
            nblk = w_ // 128
            def htr(t_, nblk=nblk):
                insts = []
                for b in range(nblk):
                    insts.append(t_.transpose(
                        psTR[:, b * 128:(b + 1) * 128],
                        hq[:, b * 128:(b + 1) * 128], idt[:]))
                return insts[-1]
            e_htr = p.step("tensor", htr, wait=[e_hq, last_ps_reader[0]])
            def evhq(v, nblk=nblk):
                src = psTR[:, 0:nblk * 128].rearrange("p (c q) -> p c q", q=128)
                v.tensor_copy(hqT[:, 0:nblk, :], src)
                return v.tensor_copy(jnk[:, 0:nblk],
                                     hqT[:, 0:nblk, 127:128]
                                     .rearrange("p c q -> p (c q)"))
            e_hqT_ev = p.step("vector", evhq, wait=[e_htr, e_down_last])
            last_ps_reader[0] = e_hqT_ev

            def dmm(t_, si=si, s0=s0, nblk=nblk):
                last = None
                for b in range(nblk):
                    ic = (s0 // 128) + b
                    for n in range(2):
                        last = t_.matmul(psD[:, n * 512:(n + 1) * 512],
                                         hqT[:, b, :],
                                         wdT[:, ic, n * 512:(n + 1) * 512],
                                         start=(si == 0 and b == 0),
                                         stop=(si == NSL - 1 and b == nblk - 1))
                return last
            e_down_last = p.step("tensor", dmm,
                                 wait=[e_hqT_ev, ev_res.get(t - 1)])
        ev_down_done[t] = e_down_last

        def res(ve, par=par):
            return ve.scalar_tensor_tensor(xout[par][:], psD[:], 0.5,
                                           xt[par][:], ALU.mult, ALU.add)
        ev_res[t] = p.step("vector", res,
                           wait=[e_down_last, ev_store.get(t - 2)])
        ev_store[t] = p.step("sync",
                             lambda s, yg=yg, par=par: s.dma_start(
                                 out=yg, in_=xout[par][:]),
                             wait=[ev_res[t]], dma=True)
    sbctx.close()
    return ev_store[T - 1]


def build_ffn1(NTOK=4096, I=2816):
    nc = bass.Bass("TRN2", target_bir_lowering=False)
    ctx = ExitStack()
    io = {}
    io["x"] = nc.dram_tensor("x", [NTOK, D], F32, kind="ExternalInput").ap()
    io["y"] = nc.dram_tensor("y", [NTOK, D], F32, kind="ExternalOutput").ap()
    io["wg"] = nc.dram_tensor("wg", [I, D], F32, kind="ExternalInput").ap()
    io["wu"] = nc.dram_tensor("wu", [I, D], F32, kind="ExternalInput").ap()
    io["wd"] = nc.dram_tensor("wd", [D, I], F32, kind="ExternalInput").ap()
    io["ident"] = nc.dram_tensor("ident", [128, 128], F32,
                                 kind="ExternalInput").ap()
    p = Prog(nc, ctx)
    ev_last = emit_ffn1(nc, ctx, p, io, NTOK=NTOK, I=I)
    p.wait_only("sync", [ev_last])
    p.build()
    ctx.close()
    return nc


def emit_attn(nc, ctx, p, io, NTOK=4096, NH=16, HD=64, causal=False,
              rope=False, biases=False, prefix="a", debug=False):
    T = NTOK // 128
    NDC = D // 128
    NQB = NTOK // 512
    NKC = NTOK // 128
    x, y, ident = io["x"], io["y"], io["ident"]

    sbctx = ExitStack()
    sb = lambda name, shape, dt: sbctx.enter_context(
        nc.sbuf_tensor(prefix + name, shape, dt))
    ps_ = lambda name, shape, dt: sbctx.enter_context(
        nc.psum_tensor(prefix + name, shape, dt))
    dram = lambda name, shape, dt: nc.dram_tensor(prefix + name, shape, dt,
                                                  kind="Internal").ap()

    idt32 = sb("_idt32", [128, 128], F32)
    idt = sb("_idt", [128, 128], BF16)
    ones1 = sb("_ones1", [1, 128], BF16)
    ones1fb = sb("_ones1fb", [1, 128], F32)
    zrow = sb("_zrow", [1, 128], BF16)
    zrowf = sb("_zrowf", [1, 128], F32)
    onesf = sb("_onesf", [1, 64], F32)
    jnk = sb("_jnk", [128, 16], F32)
    ones1f = sb("_ones1f", [1, 128], F32)

    wT_h = {w: dram(f"_w{w}T_h", [128, NDC, D], BF16) for w in "qkvo"}
    qT_h = dram("_qT_h", [128, NDC, NTOK], BF16)

    e_id32 = p.step("sync", lambda s: s.dma_start(out=idt32[:], in_=ident),
                    dma=True)
    e_id = p.step("vector", lambda v: v.tensor_copy(idt[:], idt32[:]),
                  wait=[e_id32])
    p.step("gpsimd", lambda g: g.memset(ones1[:], 1.0), inc=False)
    p.step("gpsimd", lambda g: g.memset(ones1f[:], 1.0), inc=False)
    e_onesf = p.step("gpsimd", lambda g: g.memset(onesf[:], 1.0))

    with ExitStack() as prepctx:
        psb = lambda name, shape, dt: prepctx.enter_context(
            nc.sbuf_tensor(prefix + name, shape, dt))
        pps = lambda name, shape, dt: prepctx.enter_context(
            nc.psum_tensor(prefix + name, shape, dt))
        wst = [psb(f"_wst{i}", [128, D], BF16) for i in range(2)]
        tb = [psb(f"_tb{i}", [128, NDC, 128], BF16) for i in range(2)]
        psW = [pps(f"_psW{i}", [128, 1024], BF16) for i in range(2)]
        ev_tr = {}
        ev_ev = {}
        ev_out = {}
        ev_ld = {}
        jobs = [(w, r) for w in "qkvo" for r in range(NDC)]
        for j, (w, r) in enumerate(jobs):
            par = j % 2
            wsrc = io["W" + w]
            ev_ld[j] = p.step(
                "gpsimd",
                lambda g, wsrc=wsrc, r=r, par=par: g.dma_start(
                    out=wst[par][:], in_=wsrc[r * 128:(r + 1) * 128, :]),
                wait=[ev_tr.get(j - 2)], dma=True)

            def wtr(t_, par=par):
                insts = []
                for c in range(NDC):
                    insts.append(t_.transpose(
                        psW[par][:, c * 128:(c + 1) * 128],
                        wst[par][:, c * 128:(c + 1) * 128], idt[:]))
                return insts[-1]
            ev_tr[j] = p.step("tensor", wtr,
                              wait=[ev_ld[j], e_id, ev_ev.get(j - 2)])
            def wev(v, par=par):
                src = psW[par][:].rearrange("p (c q) -> p c q", q=128)
                v.tensor_copy(tb[par][:], src)
                return v.tensor_copy(jnk[:, 0:NDC],
                                     tb[par][:, :, 127:128]
                                     .rearrange("p c q -> p (c q)"))
            ev_ev[j] = p.step("vector", wev, wait=[ev_tr[j], ev_out.get(j - 2)])
            ev_out[j] = p.step(
                "sync",
                lambda s, w=w, r=r, par=par: s.dma_start(
                    out=wT_h[w][:, :, r * 128:(r + 1) * 128], in_=tb[par][:]),
                wait=[ev_ev[j]], dma=True)
        e_prep = [ev_out[len(jobs) - 1], ev_tr[len(jobs) - 1]]

    st = sb("_stats", [128, 8, 2], F32)

    def stb(row, par):
        return st[:, row, par:par + 1]

    if biases:
        bq = sb("_bq", [1, D], F32)
        bk = sb("_bk", [1, D], F32)
        bv = sb("_bv", [1, D], F32)
        bo = sb("_bo", [1, D], F32)
        e_bias = None
        for name, buf in (("bq", bq), ("bk", bk), ("bv", bv), ("bo", bo)):
            e_bias = p.step("sync",
                            lambda s, name=name, buf=buf: s.dma_start(
                                out=buf[:], in_=io[name].unsqueeze(0)),
                            dma=True)
    else:
        bq = bk = bv = bo = None
        e_bias = None

    xnT_h = dram("_xnT_h", [128, NDC, NTOK], BF16)
    evA_last = []
    with ExitStack() as actx:
        asb = lambda name, shape, dt: actx.enter_context(
            nc.sbuf_tensor(prefix + name, shape, dt))
        aps = lambda name, shape, dt: actx.enter_context(
            nc.psum_tensor(prefix + name, shape, dt))
        wqT = asb("_wqT", [128, NDC, D], BF16)
        xtl = [asb(f"_xtl{i}", [128, D], F32) for i in range(2)]
        xnb = [asb(f"_xnb{i}", [128, D], BF16) for i in range(2)]
        sqln = asb("_sqln", [128, 2, D], F32)
        xnTt = [asb(f"_xnTt{i}", [128, NDC, 128], BF16) for i in range(2)]
        qkb = [asb(f"_qkb{i}", [128, D], BF16) for i in range(2)]
        qTt = [asb(f"_qTt{i}", [128, NDC, 128], BF16) for i in range(2)]
        qf = asb("_qf", [128, D], F32)
        if rope:
            rot = asb("_rot", [128, D], F32)
            ctile = [asb(f"_ct{i}", [128, D], F32) for i in range(2)]
            stile = [asb(f"_st{i}", [128, D], F32) for i in range(2)]
        psP = [aps(f"_psP{i}", [128, D], F32) for i in range(2)]
        psTR = [aps(f"_psTR{i}", [128, 1024], BF16) for i in range(2)]

        e_wq = p.step("sync", lambda s: s.dma_start(out=wqT[:], in_=wT_h["q"]),
                      wait=e_prep, dma=True)

        ev_red = {}
        ev_xtr = {}
        ev_evxnT = {}
        ev_xndma = {}
        ev_qmm = {}
        ev_qev = {}
        ev_qtr = {}
        ev_qdma = {}
        last_ps = [None]
        ev_ropeq = {}
        for t in range(T):
            par = t % 2
            e_x = p.step("sync",
                         lambda s, t=t, par=par: s.dma_start(
                             out=xtl[par][:], in_=x[t * 128:(t + 1) * 128, :]),
                         wait=[ev_evxnT.get(t - 2)], dma=True)
            if rope:
                e_c = p.step("sync",
                             lambda s, t=t, par=par: s.dma_start(
                                 out=ctile[par][:],
                                 in_=io["cos"][t * 128:(t + 1) * 128, :]),
                             wait=[ev_ropeq.get(t - 2)], dma=True)
                e_s = p.step("sync",
                             lambda s, t=t, par=par: s.dma_start(
                                 out=stile[par][:],
                                 in_=io["sin"][t * 128:(t + 1) * 128, :]),
                             dma=True)
            else:
                e_c = e_s = None
            bufs = dict(sq=sqln[:, par, :], sum=stb(0, par), sumsq=stb(1, par),
                        mean=stb(2, par), msq=stb(3, par), vpe=stb(4, par),
                        rstd=stb(5, par), nmr=stb(6, par), lnv=stb(7, par),
                        sq_free=ev_red.get(t - 2), xn_free=ev_xtr.get(t - 2))
            exn, ered = emit_ln(p, xtl[par][:], bufs, [e_x], xnb[par][:])
            ev_red[t] = ered

            def xtr(t_, par=par):
                insts = []
                for c in range(NDC):
                    insts.append(t_.transpose(
                        psTR[par][:, c * 128:(c + 1) * 128],
                        xnb[par][:, c * 128:(c + 1) * 128], idt[:]))
                return insts[-1]
            ev_xtr[t] = p.step("tensor", xtr, wait=[exn, last_ps[0]])
            def evxn(v, par=par):
                src = psTR[par][:].rearrange("p (c q) -> p c q", q=128)
                v.tensor_copy(xnTt[par][:], src)
                return v.tensor_copy(jnk[:, 0:NDC],
                                     xnTt[par][:, :, 127:128]
                                     .rearrange("p c q -> p (c q)"))
            ev_evxnT[t] = p.step("vector", evxn,
                                 wait=[ev_xtr[t], ev_xndma.get(t - 2)])
            last_ps[0] = ev_evxnT[t]
            ev_xndma[t] = p.step(
                "sync",
                lambda s, t=t, par=par: s.dma_start(
                    out=xnT_h[:, :, t * 128:(t + 1) * 128], in_=xnTt[par][:]),
                wait=[ev_evxnT[t]], dma=True)

            def mmq(t_, par=par):
                if biases:
                    for n in range(2):
                        t_.matmul(psP[par][:, n * 512:(n + 1) * 512],
                                  ones1f[:], bq[:, n * 512:(n + 1) * 512],
                                  start=True, stop=False)
                last = None
                for c in range(NDC):
                    for n in range(2):
                        last = t_.matmul(psP[par][:, n * 512:(n + 1) * 512],
                                         xnTt[par][:, c, :],
                                         wqT[:, c, n * 512:(n + 1) * 512],
                                         start=(not biases and c == 0),
                                         stop=(c == NDC - 1))
                return last
            e_mm = p.step("tensor", mmq,
                          wait=[ev_evxnT[t], e_wq, e_bias, ev_qev.get(t - 2)])
            if rope:
                def evr(v, par=par):
                    v.tensor_copy(qf[:], psP[par][:])
                    ro = rot[:].rearrange("p (a two) -> p a two", two=2)
                    qfv = qf[:].rearrange("p (a two) -> p a two", two=2)
                    v.tensor_scalar_mul(ro[:, :, 0:1], qfv[:, :, 1:2], -1.0)
                    v.tensor_copy(ro[:, :, 1:2], qfv[:, :, 0:1])
                    v.tensor_mul(qf[:], qf[:], ctile[par][:])
                    v.tensor_mul(rot[:], rot[:], stile[par][:])
                    v.tensor_add(qkb[par][:], qf[:], rot[:])
                    return v.tensor_copy(jnk[:, 0:1], qkb[par][:, 1023:1024])
                e_ev = p.step("vector", evr,
                              wait=[e_mm, e_c, e_s, ev_qtr.get(t - 2)])
                ev_ropeq[t] = e_ev
            else:
                def evq0(v, par=par):
                    v.tensor_copy(qkb[par][:], psP[par][:])
                    return v.tensor_copy(jnk[:, 0:1], qkb[par][:, 1023:1024])
                e_ev = p.step("vector", evq0,
                              wait=[e_mm, ev_qtr.get(t - 2)])
            ev_qev[t] = e_ev

            def qtr(t_, par=par):
                insts = []
                for c in range(NDC):
                    insts.append(t_.transpose(
                        psTR[par][:, c * 128:(c + 1) * 128],
                        qkb[par][:, c * 128:(c + 1) * 128], idt[:]))
                return insts[-1]
            e_tr = p.step("tensor", qtr, wait=[e_ev, last_ps[0]])
            ev_qtr[t] = e_tr
            def evq(v, par=par):
                src = psTR[par][:].rearrange("p (c q) -> p c q", q=128)
                v.tensor_copy(qTt[par][:], src)
                return v.tensor_copy(jnk[:, 0:NDC],
                                     qTt[par][:, :, 127:128]
                                     .rearrange("p c q -> p (c q)"))
            e_evq = p.step("vector", evq, wait=[e_tr, ev_qdma.get(t - 2)])
            ev_qdma[t] = p.step(
                "sync",
                lambda s, t=t, par=par: s.dma_start(
                    out=qT_h[:, :, t * 128:(t + 1) * 128], in_=qTt[par][:]),
                wait=[e_evq], dma=True)
            last_ps[0] = e_evq

        evA1 = [ev_qdma[T - 1], ev_xndma[T - 1], last_ps[0]]

    KT = sb("_KT", [128, NDC, NTOK], BF16)
    Vb = sb("_Vb", [128, NKC, NH * (HD + 2)], BF16)
    ctxT = sb("_ctxT", [128, NDC, 512], BF16)
    e_vones = p.step("gpsimd", lambda g: (g.memset(
        Vb[:].rearrange("p k (h c) -> p k h c", c=HD + 2)[:, :, :, HD:HD + 1],
        1.0), g.memset(
        Vb[:].rearrange("p k (h c) -> p k h c", c=HD + 2)[:, :, :,
                        HD + 1:HD + 2], 0.0))[-1], wait=evA1)

    with ExitStack() as actx:
        asb = lambda name, shape, dt: actx.enter_context(
            nc.sbuf_tensor(prefix + name, shape, dt))
        aps = lambda name, shape, dt: actx.enter_context(
            nc.psum_tensor(prefix + name, shape, dt))
        wkT = asb("_wkT", [128, NDC, D], BF16)
        wvT = asb("_wvT", [128, NDC, D], BF16)
        xnTt = [asb(f"_xnTt2{i}", [128, NDC, 128], BF16) for i in range(2)]
        qkb = [asb(f"_qkb2{i}", [128, D], BF16) for i in range(2)]
        if rope:
            qf = asb("_qf2", [128, D], F32)
            rot = asb("_rot2", [128, D], F32)
            ctile = [asb("_ct2", [128, D], F32)] * 2
            stile = [asb("_st2", [128, D], F32)] * 2
        psP = [aps(f"_psP2{i}", [128, D], F32) for i in range(2)]
        psTR = [aps(f"_psTR2{i}", [128, 1024], BF16) for i in range(2)]

        e_wk = p.step("sync", lambda s: s.dma_start(out=wkT[:], in_=wT_h["k"]),
                      wait=evA1, dma=True)
        e_wv = p.step("sync", lambda s: s.dma_start(out=wvT[:], in_=wT_h["v"]),
                      dma=True)

        ev_xnl = {}
        ev_kev = {}
        ev_ktr = {}
        ev_kevK = {}
        ev_vev = {}
        ev_ropeq = {}
        ev_mmk = {}
        last_ps = [None]
        for t in range(T):
            par = t % 2
            e_xn = p.step("sync",
                          lambda s, t=t, par=par: s.dma_start(
                              out=xnTt[par][:],
                              in_=xnT_h[:, :, t * 128:(t + 1) * 128]),
                          wait=evA1 + [ev_mmk.get(t - 2)], dma=True)
            if rope:
                e_c = p.step("sync",
                             lambda s, t=t, par=par: s.dma_start(
                                 out=ctile[par][:],
                                 in_=io["cos"][t * 128:(t + 1) * 128, :]),
                             wait=evA1 + [ev_ropeq.get(t - 2)], dma=True)
                e_s = p.step("sync",
                             lambda s, t=t, par=par: s.dma_start(
                                 out=stile[par][:],
                                 in_=io["sin"][t * 128:(t + 1) * 128, :]),
                             dma=True)
            else:
                e_c = e_s = None

            def mmk(t_, par=par):
                if biases:
                    for n in range(2):
                        t_.matmul(psP[0][:, n * 512:(n + 1) * 512],
                                  ones1f[:], bk[:, n * 512:(n + 1) * 512],
                                  start=True, stop=False)
                last = None
                for c in range(NDC):
                    for n in range(2):
                        last = t_.matmul(psP[0][:, n * 512:(n + 1) * 512],
                                         xnTt[par][:, c, :],
                                         wkT[:, c, n * 512:(n + 1) * 512],
                                         start=(not biases and c == 0),
                                         stop=(c == NDC - 1))
                return last
            e_mmk = p.step("tensor", mmk,
                           wait=[e_xn, e_wk, e_bias, ev_kev.get(t - 1)])
            if rope:
                def evrk(v, par=par):
                    v.tensor_copy(qf[:], psP[0][:])
                    ro = rot[:].rearrange("p (a two) -> p a two", two=2)
                    qfv = qf[:].rearrange("p (a two) -> p a two", two=2)
                    v.tensor_scalar_mul(ro[:, :, 0:1], qfv[:, :, 1:2], -1.0)
                    v.tensor_copy(ro[:, :, 1:2], qfv[:, :, 0:1])
                    v.tensor_mul(qf[:], qf[:], ctile[par][:])
                    v.tensor_mul(rot[:], rot[:], stile[par][:])
                    v.tensor_add(qkb[par][:], qf[:], rot[:])
                    return v.tensor_copy(jnk[:, 0:1], qkb[par][:, 1023:1024])
                e_ev = p.step("vector", evrk,
                              wait=[e_mmk, e_c, e_s, ev_ktr.get(t - 2)])
                ev_ropeq[t] = e_ev
            else:
                def evk0(v, par=par):
                    v.tensor_copy(qkb[par][:], psP[0][:])
                    return v.tensor_copy(jnk[:, 0:1], qkb[par][:, 1023:1024])
                e_ev = p.step("vector", evk0,
                              wait=[e_mmk, ev_ktr.get(t - 2)])
            ev_kev[t] = e_ev

            def ktr(t_, par=par):
                insts = []
                for c in range(NDC):
                    insts.append(t_.transpose(
                        psTR[par][:, c * 128:(c + 1) * 128],
                        qkb[par][:, c * 128:(c + 1) * 128], idt[:]))
                return insts[-1]
            e_tr = p.step("tensor", ktr, wait=[e_ev, last_ps[0]])
            ev_ktr[t] = e_tr
            def evk(v, par=par, t=t):
                src = psTR[par][:].rearrange("p (c q) -> p c q", q=128)
                v.tensor_copy(KT[:, :, t * 128:(t + 1) * 128], src)
                return v.tensor_copy(jnk[:, 0:NDC],
                                     KT[:, :, t * 128 + 127:t * 128 + 128]
                                     .rearrange("p c q -> p (c q)"))
            e_evk = p.step("vector", evk, wait=[e_tr])
            ev_kevK[t] = e_evk
            last_ps[0] = e_evk

            def mmv(t_, par=par):
                if biases:
                    for n in range(2):
                        t_.matmul(psP[1][:, n * 512:(n + 1) * 512],
                                  ones1f[:], bv[:, n * 512:(n + 1) * 512],
                                  start=True, stop=False)
                last = None
                for c in range(NDC):
                    for n in range(2):
                        last = t_.matmul(psP[1][:, n * 512:(n + 1) * 512],
                                         xnTt[par][:, c, :],
                                         wvT[:, c, n * 512:(n + 1) * 512],
                                         start=(not biases and c == 0),
                                         stop=(c == NDC - 1))
                return last
            e_mmv = p.step("tensor", mmv,
                           wait=[e_xn, e_wv, e_bias, ev_vev.get(t - 1)])
            ev_mmk[t] = e_mmv
            def evv(v, t=t):
                src = psP[1][:].rearrange("p (h c) -> p h c", c=HD)
                dst = Vb[:, t, :].rearrange("p (h c) -> p h c",
                                            c=HD + 2)[:, :, 0:HD]
                v.tensor_copy(dst, src)
                return v.tensor_copy(jnk[:, 0:16],
                                     Vb[:, t, :].rearrange(
                                         "p (h c) -> p h c",
                                         c=HD + 2)[:, :, HD - 1:HD]
                                     .rearrange("p h c -> p (h c)"))
            ev_vev[t] = p.step("vector", evv, wait=[e_mmv, e_vones])
            if debug and t == T - 1:
                dbv = nc.dram_tensor(prefix + "_dbg_bv", [1, D], BF16,
                                     kind="ExternalOutput").ap()
                p.step("sync", lambda s: s.dma_start(out=dbv, in_=bv[:]),
                       wait=[e_mmv], dma=True)
                dpsp = nc.dram_tensor(prefix + "_dbg_psP", [128, D], F32,
                                      kind="ExternalOutput").ap()
                psp_sb = asb("_dbg_psp_sb", [128, D], F32)
                e_dp2 = p.step("vector",
                               lambda v: v.tensor_copy(psp_sb[:], psP[1][:]),
                               wait=[e_mmv])
                p.step("sync",
                       lambda s: s.dma_start(out=dpsp, in_=psp_sb[:]),
                       wait=[e_dp2], dma=True)
        evA_last = [ev_vev[T - 1], ev_kevK[T - 1], last_ps[0]]

    with ExitStack() as bctx:
        bsb = lambda name, shape, dt: bctx.enter_context(
            nc.sbuf_tensor(prefix + name, shape, dt))
        bps = lambda name, shape, dt: bctx.enter_context(
            nc.psum_tensor(prefix + name, shape, dt))
        woT = bsb("_woT", [128, NDC, D], BF16)
        nqb_buf = 1 if causal else 2
        QTs = [bsb(f"_QTs{i}", [128, NDC, 512], BF16) for i in range(2)]
        PT = [bsb(f"_PT{i}", [128, 512], BF16) for i in range(2)]
        cst = [bsb("_cst", [64, 512], F32)] * 2
        rsb = bsb("_rsb", [1, 512], F32)
        xrl = [bsb("_xrl", [128, D], F32)] * 2
        xob = [bsb("_xob", [128, D], F32)] * 2
        cmask = bsb("_cmask", [128, 4, 512], BF16) if causal else None
        nb10 = bsb("_nb10", [128, 1], F32)
        psS = [bps(f"_psS{i}", [128, 512], F32) for i in range(2)]
        psC = [bps(f"_psC{i}", [65, 512], F32) for i in range(2)]
        psR = bps("_psR", [64, 512], F32)
        psO = bps("_psO", [128, D], F32)

        e_nb = p.step("gpsimd", lambda g: g.memset(nb10[:], -10.0))
        e_wo = p.step("sync", lambda s: s.dma_start(out=woT[:], in_=wT_h["o"]),
                      wait=evA_last, dma=True)
        if causal:
            e_cm = p.step("sync",
                          lambda s: s.dma_start(out=cmask[:], in_=io["cmask"]),
                          wait=evA_last, dma=True)
        else:
            e_cm = None

        ev_qt = {}
        ev_store = {}
        ev_res = {}
        ev_ctx = {}
        ev_exp = {}
        ev_pv = {}
        ev_proj = {}
        ev_cst = {}
        ev_psRmm = {}
        ev_rec = {}
        e_pvl_prev = [None]
        for qb in range(NQB):
            pq = qb % 2
            qt_free = ev_proj.get((qb - 1, 3))
            ev_qt[qb] = p.step(
                "sync",
                lambda s, qb=qb, pq=pq: s.dma_start(
                    out=QTs[pq][:], in_=qT_h[:, :, qb * 512:(qb + 1) * 512]),
                wait=evA_last + [qt_free], dma=True)
            kcmax = 4 * qb + 4 if causal else NKC
            for h in range(NH):
                ph = h % 2
                hb = (h % 2) * 64
                hc = h // 2
                for kc in range(kcmax):
                    pk = kc % 2

                    def smm(t_, hb=hb, hc=hc, kc=kc, pk=pk, pq=pq):
                        return t_.matmul(
                            psS[pk][:],
                            KT[hb:hb + 64, hc, kc * 128:(kc + 1) * 128],
                            QTs[pq][hb:hb + 64, hc, :],
                            start=True, stop=True)
                    sw = [ev_qt[qb], ev_exp.get((qb, h, kc - 2))]
                    if kc < 2:
                        sw.append(ev_cst.get((qb, h - 1)) if h > 0 else
                                  (ev_cst.get((qb - 1, NH - 1)) if qb > 0
                                   else e_wo))
                    e_smm = p.step("tensor", smm, wait=sw)

                    def sexp(sc, pk=pk):
                        return sc.activation(PT[pk][:], psS[pk][:], AF.Exp,
                                             bias=nb10[:], scale=0.125)
                    ptfree = ev_pv.get((qb, h, kc - 2))
                    if kc < 2:
                        ptfree = e_pvl_prev[0]
                    e_exp = p.step("scalar", sexp, wait=[e_smm, ptfree, e_nb])
                    if causal and kc >= 4 * qb:
                        off = kc - 4 * qb
                        def pmask(v, pk=pk, off=off):
                            v.tensor_mul(PT[pk][:], PT[pk][:],
                                         cmask[:, off, :])
                            return v.tensor_copy(jnk[:, 0:1],
                                                 PT[pk][:, 511:512])
                        e_exp = p.step("vector", pmask, wait=[e_exp, e_cm])
                    ev_exp[(qb, h, kc)] = e_exp

                    def pvmm(t_, h=h, kc=kc, pk=pk, ph=ph, kcmax=kcmax):
                        return t_.matmul(
                            psC[ph][:],
                            Vb[:, kc, h * (HD + 2):h * (HD + 2) + HD + 1],
                            PT[pk][:], start=(kc == 0), stop=(kc == kcmax - 1))
                    pvw = [e_exp]
                    if kc == 0:
                        pvw += [ev_cst.get((qb, h - 2)), ev_rec.get((qb, h - 2))]
                        if qb > 0 and h < 2:
                            pvw += [ev_cst.get((qb - 1, NH - 2 + ph)),
                                    ev_rec.get((qb - 1, NH - 2 + ph))]
                    ev_pv[(qb, h, kc)] = p.step("tensor", pvmm, wait=pvw)
                e_pvl = ev_pv[(qb, h, kcmax - 1)]
                e_pvl_prev[0] = e_pvl
                def sums(v, ph=ph):
                    return v.reciprocal(rsb[:], psC[ph][64:65, :])
                e_rec = p.step("vector", sums,
                               wait=[e_pvl, ev_psRmm.get((qb, h - 1)) or
                                     (ev_psRmm.get((qb - 1, NH - 1)) if qb > 0
                                      else None)])
                ev_rec[(qb, h)] = e_rec
                def rmm(t_):
                    return t_.matmul(psR[:], onesf[:, 0:64], rsb[:],
                                     start=True, stop=True)
                e_rmm = p.step("tensor", rmm,
                               wait=[e_rec, e_onesf,
                                     ev_ctx.get((qb, h - 1)) or
                                     (ev_ctx.get((qb - 1, NH - 1)) if qb > 0
                                      else None)])
                ev_psRmm[(qb, h)] = e_rmm
                e_cst = p.step("scalar",
                               lambda sc, ph=ph: sc.activation(
                                   cst[ph][:], psC[ph][0:64, :], AF.Identity),
                               wait=[e_pvl, e_rec,
                                     ev_ctx.get((qb, h - 1)) or
                                     (ev_ctx.get((qb - 1, NH - 1))
                                      if qb > 0 else None)])
                ev_cst[(qb, h)] = e_cst

                def ctxmul(v, ph=ph, hb=hb, hc=hc):
                    v.tensor_tensor(ctxT[hb:hb + 64, hc, :], cst[ph][:],
                                    psR[:], ALU.mult)
                    return v.tensor_copy(jnk[hb:hb + 64, 0:1],
                                         ctxT[hb:hb + 64, hc, 511:512])
                ev_ctx[(qb, h)] = p.step(
                    "vector", ctxmul,
                    wait=[e_cst, e_rmm, ev_proj.get((qb - 1, 3))])

            for j in range(4):
                def pmm(t_, j=j):
                    if biases:
                        for n in range(2):
                            t_.matmul(psO[:, n * 512:(n + 1) * 512],
                                      ones1f[:], bo[:, n * 512:(n + 1) * 512],
                                      start=True, stop=False)
                    last = None
                    for c in range(NDC):
                        for n in range(2):
                            last = t_.matmul(psO[:, n * 512:(n + 1) * 512],
                                             ctxT[:, c, j * 128:(j + 1) * 128],
                                             woT[:, c, n * 512:(n + 1) * 512],
                                             start=(not biases and c == 0),
                                             stop=(c == NDC - 1))
                    return last
                pw = [ev_ctx[(qb, NH - 1)], e_wo, ev_res.get((qb, j - 1))]
                ev_proj[(qb, j)] = p.step("tensor", pmm, wait=pw)
                tt0 = qb * 4 + j
                e_xr = p.step("sync",
                              lambda s, tt0=tt0, j=j: s.dma_start(
                                  out=xrl[j % 2][:],
                                  in_=x[tt0 * 128:(tt0 + 1) * 128, :]),
                              wait=[ev_res.get((qb, j - 1))], dma=True)
                def res(v, j=j):
                    return v.tensor_tensor(xob[j % 2][:], psO[:], xrl[j % 2][:],
                                           ALU.add)
                ev_res[(qb, j)] = p.step("vector", res,
                                         wait=[ev_proj[(qb, j)], e_xr,
                                               ev_store.get((qb, j - 1))])
                ev_store[(qb, j)] = p.step(
                    "sync",
                    lambda s, tt0=tt0, j=j: s.dma_start(
                        out=y[tt0 * 128:(tt0 + 1) * 128, :], in_=xob[j % 2][:]),
                    wait=[ev_res[(qb, j)]], dma=True)
        if debug:
            dpsr = nc.dram_tensor(prefix + "_dbg_psR", [64, 512], F32,
                                  kind="ExternalOutput").ap()
            dcst = nc.dram_tensor(prefix + "_dbg_cst", [64, 512], F32,
                                  kind="ExternalOutput").ap()
            psr_sb = bsb("_dbg_psr_sb", [64, 512], F32)
            e_dbgev = p.step("vector", lambda v: v.tensor_copy(psr_sb[:], psR[:]),
                             wait=[ev_ctx[(NQB - 1, NH - 1)]])
            p.step("sync", lambda s: s.dma_start(out=dpsr, in_=psr_sb[:]),
                   wait=[e_dbgev], dma=True)
            p.step("sync", lambda s: s.dma_start(out=dcst, in_=cst[1][:]),
                   wait=[e_dbgev], dma=True)
            dctx = nc.dram_tensor(prefix + "_dbg_ctxT", [128, NDC, 512], BF16,
                                  kind="ExternalOutput").ap()
            drsb = nc.dram_tensor(prefix + "_dbg_rsb", [1, 512], F32,
                                  kind="ExternalOutput").ap()
            dkt = nc.dram_tensor(prefix + "_dbg_KT", [128, NDC, NTOK], BF16,
                                 kind="ExternalOutput").ap()
            dvb = nc.dram_tensor(prefix + "_dbg_Vb", [128, NTOK // 128,
                                                      NH * (HD + 2)], BF16,
                                 kind="ExternalOutput").ap()
            e_d = p.step("sync", lambda s: s.dma_start(out=dctx, in_=ctxT[:]),
                         wait=[ev_store[(NQB - 1, 3)],
                               ev_ctx[(NQB - 1, NH - 1)]], dma=True)
            e_d = p.step("sync", lambda s: s.dma_start(out=drsb, in_=rsb[:]),
                         dma=True)
            e_d = p.step("sync", lambda s: s.dma_start(out=dkt, in_=KT[:]),
                         dma=True)
            e_d = p.step("sync", lambda s: s.dma_start(out=dvb, in_=Vb[:]),
                         dma=True)
            return e_d
    sbctx.close()
    return ev_store[(NQB - 1, 3)]


def build_attn(NTOK=4096, causal=False, rope=False, biases=False, debug=False):
    nc = bass.Bass("TRN2", target_bir_lowering=False)
    ctx = ExitStack()
    io = {}
    io["x"] = nc.dram_tensor("x", [NTOK, D], F32, kind="ExternalInput").ap()
    io["y"] = nc.dram_tensor("y", [NTOK, D], F32, kind="ExternalOutput").ap()
    for w in "qkvo":
        io["W" + w] = nc.dram_tensor(f"W{w}", [D, D], F32,
                                     kind="ExternalInput").ap()
        if biases:
            io["b" + w] = nc.dram_tensor(f"b{w}", [D], F32,
                                         kind="ExternalInput").ap()
    if rope:
        io["cos"] = nc.dram_tensor("cos", [NTOK, D], F32,
                                   kind="ExternalInput").ap()
        io["sin"] = nc.dram_tensor("sin", [NTOK, D], F32,
                                   kind="ExternalInput").ap()
    if causal:
        io["cmask"] = nc.dram_tensor("cmask", [128, 4, 512], BF16,
                                     kind="ExternalInput").ap()
    io["ident"] = nc.dram_tensor("ident", [128, 128], F32,
                                 kind="ExternalInput").ap()
    p = Prog(nc, ctx)
    ev_last = emit_attn(nc, ctx, p, io, NTOK=NTOK, causal=causal, rope=rope,
                        biases=biases, debug=debug)
    p.wait_only("sync", [ev_last])
    p.build()
    ctx.close()
    return nc


def build_fused(NTOK=4096):
    nc = bass.Bass("TRN2", target_bir_lowering=False)
    ctx = ExitStack()
    p = Prog(nc, ctx)
    ein = lambda n, shp: nc.dram_tensor(n, shp, F32, kind="ExternalInput").ap()
    x0 = ein("x", [NTOK, D])
    y = nc.dram_tensor("y", [NTOK, D], F32, kind="ExternalOutput").ap()
    xb = [nc.dram_tensor(f"xb{i}", [NTOK, D], F32, kind="Internal").ap()
          for i in range(5)]
    ident = ein("ident", [128, 128])
    moe_io = dict(rw=ein("rw", [32, D]), rb=ein("rb", [32]),
                  w1=ein("w1", [32, D, 256]), b1=ein("b1", [32, 256]),
                  w2=ein("w2", [32, 256, D]), b2=ein("b2", [32, D]),
                  ident=ident, ecol=ein("ecol", [32, 1]))
    a0_io = dict(Wq=ein("Wq0", [D, D]), Wk=ein("Wk0", [D, D]),
                 Wv=ein("Wv0", [D, D]), Wo=ein("Wo0", [D, D]),
                 cos=ein("cos", [NTOK, D]), sin=ein("sin", [NTOK, D]),
                 cmask=nc.dram_tensor("cmask", [128, 4, 512], BF16,
                                      kind="ExternalInput").ap(),
                 ident=ident)
    f1_io = dict(wg=ein("wg", [2816, D]), wu=ein("wu", [2816, D]),
                 wd=ein("wd", [D, 2816]), ident=ident)
    a1_io = dict(Wq=ein("Wq1", [D, D]), bq=ein("bq1", [D]),
                 Wk=ein("Wk1", [D, D]), bk=ein("bk1", [D]),
                 Wv=ein("Wv1", [D, D]), bv=ein("bv1", [D]),
                 Wo=ein("Wo1", [D, D]), bo=ein("bo1", [D]), ident=ident)

    emit_moe_top1(nc, ctx, p, dict(x=x0, y=xb[0], **moe_io), NTOK=NTOK,
                  prefix="m0")
    p.barrier()
    emit_attn(nc, ctx, p, dict(x=xb[0], y=xb[1], **a0_io), NTOK=NTOK,
              causal=True, rope=True, biases=False, prefix="a0")
    p.barrier()
    emit_moe_top1(nc, ctx, p, dict(x=xb[1], y=xb[2], **moe_io), NTOK=NTOK,
                  prefix="m1")
    p.barrier()
    emit_ffn1(nc, ctx, p, dict(x=xb[2], y=xb[3], **f1_io), NTOK=NTOK,
              prefix="f0")
    p.barrier()
    emit_attn(nc, ctx, p, dict(x=xb[3], y=xb[4], **a1_io), NTOK=NTOK,
              causal=False, rope=False, biases=True, prefix="a1")
    p.barrier()
    ev = emit_ffn1(nc, ctx, p, dict(x=xb[4], y=y, **f1_io), NTOK=NTOK,
                   prefix="f1")
    p.wait_only("sync", [ev])
    p.build()
    ctx.close()
    return nc


def emit_moe_top1(nc, ctx, p, io, NTOK=4096, NEXP=32, HEXP=256, CAP=256,
                  prefix="m"):
    T = NTOK // 128
    NDC = D // 128
    NHC = HEXP // 128
    NE = NEXP
    x, y = io["x"], io["y"]
    rw, rb = io["rw"], io["rb"]
    w1, b1, w2, b2 = io["w1"], io["b1"], io["w2"], io["b2"]
    ident, ecol = io["ident"], io["ecol"]

    sbctx = ExitStack()
    sb = lambda name, shape, dt: sbctx.enter_context(
        nc.sbuf_tensor(prefix + name, shape, dt))
    ps_ = lambda name, shape, dt: sbctx.enter_context(
        nc.psum_tensor(prefix + name, shape, dt))
    dram = lambda name, shape, dt: nc.dram_tensor(prefix + name, shape, dt,
                                                  kind="Internal").ap()

    if "dbg_xg" in io:
        xg_h = io["dbg_xg"]
        yg_h = io["dbg_yg"]
    else:
        xg_h = dram("_xg", [NE * CAP, D], BF16)
        yg_h = dram("_yg", [NE * CAP, D], F32)

    xn_h = dram("_xnh", [NTOK, D], BF16)
    xn2 = [sb(f"_xn2{i}", [128, D], BF16) for i in range(2)]
    xst = [sb(f"_xst{i}", [128, D], BF16) for i in range(2)]
    xtl = [sb(f"_xtl{i}", [128, D], F32) for i in range(2)]
    xnf = [sb(f"_xnf{i}", [128, D], F32) for i in range(2)]
    sqln = sb("_sqln", [128, D], F32)
    xnT32 = [sb(f"_xnT32{i}", [128, D], F32) for i in range(2)]
    rw_st = sb("_rwst", [NEXP, D], F32)
    rwT = sb("_rwT", [128, NDC * NEXP], F32)
    rbT = sb("_rbT", [1, NEXP], F32)
    idt32 = sb("_idt32", [128, 128], F32)
    idt = sb("_idt", [128, 128], BF16)
    ones1f = sb("_ones1f", [1, 128], F32)
    zrowf = sb("_zrowf", [1, 128], F32)
    st = sb("_stats", [128, 8, 2], F32)
    lgt = sb("_lgt", [128, T, NEXP], F32)
    mx = sb("_mx", [128, T], F32)
    mx2 = sb("_mx2", [128, T], F32)
    hmall = sb("_hmall", [128, 32, NEXP], F32)
    hmce = sb("_hmce", [128, 32, NEXP], F32)
    onec = sb("_onec", [128, NEXP], F32)
    HMT = sb("_HMT", [32, NTOK], F32)
    ones32s = sb("_ones32s", [32, 1], F32)
    ONES32 = ones32s.broadcast_to([32, NTOK])
    CUM = sb("_CUM", [32, NTOK], F32)
    ecolt = sb("_ecol", [32, 1], F32)
    zcol = sb("_zcol", [32, 1], F32)
    onecol = sb("_onecol", [32, 1], F32)
    idxp = sb("_idxp", [128, 512], F32)
    idxf = idxp[:, 0:T]
    idxt = sb("_idxt", [128, T], I32)
    idxt2 = sb("_idxt2", [128, T], I32)
    jnkbig = sb("_jnkbig", [128, 512], F32)
    jnk = sb("_jnk", [128, 16], F32)
    w1sb = [sb(f"_w1sb{i}", [128, NDC, HEXP], BF16) for i in range(2)]
    w2sb = [sb(f"_w2sb{i}", [128, NHC, D], BF16) for i in range(2)]
    b1sb = [sb(f"_b1sb{i}", [1, HEXP], F32) for i in range(2)]
    b2sb = [sb(f"_b2sb{i}", [1, D], F32) for i in range(2)]
    xgt = [sb(f"_xgt{i}", [128, D], BF16) for i in range(2)]
    xnTt = [sb(f"_xnTt{i}", [128, NDC, 128], BF16) for i in range(2)]
    hsb = [sb(f"_hsb{i}", [128, HEXP], BF16) for i in range(2)]
    hT = [sb(f"_hT{i}", [128, NHC, 128], BF16) for i in range(2)]
    ysb = [sb(f"_ysb{i}", [128, D], F32) for i in range(2)]
    yt_ = [sb(f"_yt{i}", [128, D], F32) for i in range(2)]
    xot = [sb(f"_xot{i}", [128, D], F32) for i in range(2)]

    psY = ps_("_psY", [128, D], F32)
    psA = ps_("_psA", [128, 512], F32)
    psF = ps_("_psF", [32, 128], F32)
    psTR = ps_("_psTR", [128, 1024], BF16)
    psH = ps_("_psH", [128, 512], F32)
    psE = ps_("_psE", [128, D], F32)

    def stb(row, par):
        return st[:, row, par:par + 1]

    e_id32 = p.step("sync", lambda s: s.dma_start(out=idt32[:], in_=ident),
                    dma=True)
    e_id = p.step("vector", lambda v: v.tensor_copy(idt[:], idt32[:]),
                  wait=[e_id32])
    e_rw = p.step("sync", lambda s: s.dma_start(out=rw_st[:], in_=rw), dma=True)
    e_rb = p.step("sync", lambda s: s.dma_start(out=rbT[:], in_=rb.unsqueeze(0)),
                  dma=True)
    e_ec = p.step("sync", lambda s: s.dma_start(out=ecolt[:], in_=ecol),
                  dma=True)
    p.step("gpsimd", lambda g: g.memset(onec[:], 1.0), inc=False)
    p.step("gpsimd", lambda g: g.memset(zrowf[:], 0.0), inc=False)
    p.step("gpsimd", lambda g: g.memset(zcol[:], 0.0), inc=False)
    p.step("gpsimd", lambda g: g.memset(onecol[:], 1.0), inc=False)
    e_ones = p.step("gpsimd", lambda g: (g.memset(ones1f[:], 1.0),
                                         g.memset(ones32s[:], 1.0))[-1])

    def rtr(t_):
        insts = []
        for c in range(NDC):
            insts.append(t_.transpose(psY[:, c * NEXP:(c + 1) * NEXP],
                                      rw_st[:, c * 128:(c + 1) * 128],
                                      idt32[0:NEXP, 0:NEXP]))
        return insts[-1]
    e_rtr = p.step("tensor", rtr, wait=[e_id32, e_rw])
    def evrwT(v):
        v.tensor_copy(rwT[:], psY[:, 0:NDC * NEXP])
        return v.tensor_copy(jnk[:, 0:1], rwT[:, NDC * NEXP - 1:NDC * NEXP])
    e_rwT = p.step("vector", evrwT, wait=[e_rtr])

    ev_red = {}
    ev_xnc = {}
    ev_xnd = {}
    ev_t32 = {}
    ev_evx32 = {}
    ev_rt = {}
    ev_vlg = {}
    ev_hm = {}
    ev_hmT = {}
    ev_evhmT = {}
    last_psF = [None]
    for t in range(T):
        par = t % 2
        e_x = p.step("sync",
                     lambda s, t=t, par=par: s.dma_start(
                         out=xtl[par][:], in_=x[t * 128:(t + 1) * 128, :]),
                     wait=[ev_xnc.get(t - 2)], dma=True)
        bufs = dict(sq=sqln[:], sum=stb(0, par), sumsq=stb(1, par),
                    mean=stb(2, par), msq=stb(3, par), vpe=stb(4, par),
                    rstd=stb(5, par), nmr=stb(6, par), lnv=stb(7, par),
                    sq_free=ev_red.get(t - 1), xn_free=ev_t32.get(t - 2))
        exnf, ered = emit_ln(p, xtl[par][:], bufs, [e_x], xnf[par][:])
        ev_red[t] = ered
        def xnc(sc, par=par, t=t):
            return sc.activation(xn2[par][:], xnf[par][:], AF.Identity)
        ev_xnc[t] = p.step("scalar", xnc,
                           wait=[exnf, ev_xnd.get(t - 2)])
        ev_xnd[t] = p.step("sync",
                           lambda s, t=t, par=par: s.dma_start(
                               out=xn_h[t * 128:(t + 1) * 128, :],
                               in_=xn2[par][:]),
                           wait=[ev_xnc[t]], dma=True)
        def tr32(t_, par=par):
            insts = []
            for c in range(NDC):
                insts.append(t_.transpose(psY[:, c * 128:(c + 1) * 128],
                                          xnf[par][:, c * 128:(c + 1) * 128],
                                          idt32[:]))
            return insts[-1]
        t32w = [exnf, ev_evx32.get(t - 1)]
        if t == 0:
            t32w.append(e_rwT)
        ev_t32[t] = p.step("tensor", tr32, wait=t32w)
        def evx32(v, par=par):
            v.tensor_copy(xnT32[par][:], psY[:])
            return v.tensor_copy(jnk[:, 0:1], xnT32[par][:, 1023:1024])
        ev_evx32[t] = p.step("vector", evx32, wait=[ev_t32[t]])

        def rt(t_, par=par):
            t_.matmul(psA[:, 0:NEXP], ones1f[:], rbT[:], start=True,
                      stop=False)
            for c in range(NDC):
                t_.matmul(psA[:, 0:NEXP],
                          xnT32[par][:, c * 128:(c + 1) * 128],
                          rwT[:, c * NEXP:(c + 1) * NEXP],
                          start=False, stop=False)
            return t_.matmul(psA[:, 0:NEXP], zrowf[:], rbT[:],
                             start=False, stop=True)
        ev_rt[t] = p.step("tensor", rt,
                          wait=[ev_evx32[t], e_ones, e_rb,
                                ev_vlg.get(t - 1)])
        e_lgc = p.step("scalar",
                       lambda sc, t=t: sc.activation(lgt[:, t, :],
                                                     psA[:, 0:NEXP],
                                                     AF.Identity),
                       wait=[ev_rt[t]])
        ev_vlg[t] = p.step("vector",
                           lambda v, t=t: v.reduce_max(mx[:, t:t + 1],
                                                       lgt[:, t, :],
                                                       axis=AX.X),
                           wait=[e_lgc])
        e_mx2 = p.step("gpsimd",
                       lambda g_, t=t: g_.tensor_copy(mx2[:, t:t + 1],
                                                      mx[:, t:t + 1]),
                       wait=[ev_vlg[t]])
        def hmstep(v, t=t):
            return v.scalar_tensor_tensor(hmall[:, t, :], lgt[:, t, :],
                                          mx2[:, t:t + 1], onec[:],
                                          ALU.is_equal, ALU.mult)
        e_hmv = p.step("vector", hmstep, wait=[e_mx2])
        ev_hm[t] = p.step("scalar",
                          lambda sc, t=t: sc.activation(
                              hmce[:, t, :], hmall[:, t, :], AF.Identity),
                          wait=[e_hmv])

    for t in range(T):
        ev_hmT[t] = p.step("tensor",
                           lambda t_, t=t: t_.transpose(psF[:],
                                                        hmce[:, t, :],
                                                        idt32[:]),
                           wait=[ev_hm[t], last_psF[0], e_id32])
        def evhm(sc, t=t):
            return sc.activation(HMT[:, t * 128:(t + 1) * 128], psF[:],
                                 AF.Identity)
        ev_evhmT[t] = p.step("scalar", evhm, wait=[ev_hmT[t]])
        last_psF[0] = ev_evhmT[t]

    if "dbg_hmt" in io:
        p.step("sync", lambda s: s.dma_start(out=io["dbg_hmt"], in_=HMT[:]),
               wait=[ev_evhmT[T - 1]], dma=True)
    e_cum = p.step("vector",
                   lambda v: v.tensor_tensor_scan(CUM[:], ONES32[:], HMT[:],
                                                  0.0, ALU.mult, ALU.add),
                   wait=[ev_evhmT[T - 1], e_ones])
    if "dbg_cum" in io:
        p.step("sync", lambda s: s.dma_start(out=io["dbg_cum"], in_=CUM[:]),
               wait=[e_cum], dma=True)
    def destmm1(t_):
        for t in range(T):
            t_.matmul(psA[:, t:t + 1], HMT[:, t * 128:(t + 1) * 128],
                      ecolt[:], start=(t == 0), stop=False)
        return t_.matmul(psA[:, T - 1:T], HMT[:, (T - 1) * 128:T * 128],
                         zcol[:], start=False, stop=False)
    e_dm1 = p.step("tensor", destmm1, wait=[e_cum, e_ec, ev_vlg[T - 1]])
    def posmask(g):
        g.tensor_scalar(CUM[:], CUM[:], -1.0, float(CAP - 1), ALU.add, ALU.min)
        g.tensor_mul(HMT[:], HMT[:], CUM[:])
        return g.tensor_copy(jnk[0:32, 0:1], HMT[:, NTOK - 1:NTOK])
    e_mv = p.step("gpsimd", posmask, wait=[e_dm1])
    def destmm2(t_):
        for t in range(T):
            t_.matmul(psA[:, t:t + 1], HMT[:, t * 128:(t + 1) * 128],
                      onecol[:], start=False, stop=False)
        return t_.matmul(psA[:, T - 1:T], HMT[:, (T - 1) * 128:T * 128],
                         zcol[:], start=False, stop=True)
    e_dmm = p.step("tensor", destmm2, wait=[e_mv])
    e_idx = p.step("scalar",
                   lambda sc: sc.activation(idxp[:], psA[:], AF.Identity),
                   wait=[e_dmm])
    idx_h = dram("_idxh", [128, T], F32)
    e_ib1 = p.step("sync", lambda s: s.dma_start(out=idx_h, in_=idxf[:]),
                   wait=[e_idx], dma=True)
    e_idx2 = p.step("gpsimd", lambda g: g.dma_start(out=idxt2[:], in_=idx_h),
                    wait=[e_ib1], dma=True)

    if "dbg_idx" in io:
        e_di = p.step("sync", lambda s: s.dma_start(out=io["dbg_idx"],
                                                    in_=idxf[:]),
                      wait=[e_idx], dma=True)
    if "dbg_idx2" in io:
        idxf2 = sb("_idxf2", [128, T], F32)
        e_cv2 = p.step("vector", lambda v: v.tensor_copy(idxf2[:], idxt2[:]),
                       wait=[e_idx2])
        p.step("sync", lambda s: s.dma_start(out=io["dbg_idx2"], in_=idxf2[:]),
               wait=[e_cv2], dma=True)
    ev_scat = {}
    ev_xld = {}
    for t in range(T):
        par = t % 2
        ev_xld[t] = p.step(
            "gpsimd",
            lambda g, t=t, par=par: g.dma_start(
                out=xst[par][:], in_=xn_h[t * 128:(t + 1) * 128, :]),
            wait=[ev_xnd[t], ev_scat.get(t - 2)], dma=True)
        ev_scat[t] = p.step(
            "gpsimd",
            lambda g, t=t, par=par: g.indirect_dma_start(
                out=xg_h[0:128, :],
                out_offset=bass.IndirectOffsetOnAxis(idxt2[:, t:t + 1], 0),
                in_=xst[par][:], in_offset=None),
            wait=[e_idx2, ev_xld[t]], dma=True)

    ev_wl = {}
    ev_ld = {}
    ev_tx = {}
    ev_evt = {}
    ev_hmm = {}
    ev_reluh = {}
    ev_htr = {}
    ev_evh = {}
    ev_ymm = {}
    ev_reluy = {}
    ev_yst = {}
    last_psTR = [None]
    NSUB = CAP // 128
    for e in range(NE):
        wp = e % 2
        wfree = ev_ymm.get((e - 2, NSUB - 1))
        p.step("gpsimd",
               lambda g_, e=e, wp=wp: g_.dma_start(
                   out=w1sb[wp][:],
                   in_=w1[e].rearrange("(c p) h -> p c h", p=128)),
               wait=[wfree], dma=True)
        e_wl = p.step("gpsimd",
                      lambda g_, e=e, wp=wp: g_.dma_start(
                          out=w2sb[wp][:],
                          in_=w2[e].rearrange("(c p) d -> p c d", p=128)),
                      dma=True)
        p.step("sync",
               lambda s_, e=e, wp=wp: s_.dma_start(out=b1sb[wp][:],
                                                   in_=b1[e:e + 1, :]),
               dma=True)
        e_wl2 = p.step("sync",
                       lambda s_, e=e, wp=wp: s_.dma_start(
                           out=b2sb[wp][:], in_=b2[e:e + 1, :]),
                       dma=True)
        for sub in range(NSUB):
            par = (e * NSUB + sub) % 2
            row0 = e * CAP + sub * 128
            e_ld = p.step("sync",
                          lambda s, row0=row0, par=par: s.dma_start(
                              out=xgt[par][:], in_=xg_h[row0:row0 + 128, :]),
                          wait=[ev_scat.get(T - 1) if e == 0 and sub == 0
                                else None,
                                ev_tx.get((e, sub - 2)) or
                                ev_tx.get((e - 1, NSUB - 2 + (sub % 2)))],
                          dma=True)
            def tx(t_, par=par):
                insts = []
                for c in range(NDC):
                    insts.append(t_.transpose(
                        psTR[:, c * 128:(c + 1) * 128],
                        xgt[par][:, c * 128:(c + 1) * 128], idt[:]))
                return insts[-1]
            e_tx = p.step("tensor", tx, wait=[e_ld, e_id, last_psTR[0]])
            ev_tx[(e, sub)] = e_tx
            def evt(v, par=par):
                src = psTR[:].rearrange("p (c q) -> p c q", q=128)
                v.tensor_copy(xnTt[par][:], src)
                return v.tensor_copy(jnk[:, 0:NDC],
                                     xnTt[par][:, :, 127:128]
                                     .rearrange("p c q -> p (c q)"))
            e_evt = p.step("vector", evt, wait=[e_tx])
            last_psTR[0] = e_evt

            def hmm(t_, par=par, wp=wp):
                t_.matmul(psH[:, 0:HEXP], ones1f[:], b1sb[wp][:],
                          start=True, stop=False)
                for c in range(NDC):
                    t_.matmul(psH[:, 0:HEXP], xnTt[par][:, c, :],
                              w1sb[wp][:, c, :], start=False, stop=False)
                return t_.matmul(psH[:, 0:HEXP], zrowf[:], b1sb[wp][:],
                                 start=False, stop=True)
            e_hmm = p.step("tensor", hmm,
                           wait=[e_evt, e_wl, e_wl2,
                                 ev_reluh.get((e, sub - 1)) or
                                 ev_reluh.get((e - 1, NSUB - 1))])
            ev_hmm[(e, sub)] = e_hmm
            e_rh = p.step("scalar",
                          lambda sc, par=par: sc.activation(
                              hsb[par][:], psH[:, 0:HEXP], AF.Relu),
                          wait=[e_hmm,
                                ev_htr.get((e, sub - 2)) or
                                ev_htr.get((e - 1, NSUB - 2 + (sub % 2)))])
            ev_reluh[(e, sub)] = e_rh
            def htr(t_, par=par):
                insts = []
                for c in range(NHC):
                    insts.append(t_.transpose(
                        psTR[:, c * 128:(c + 1) * 128],
                        hsb[par][:, c * 128:(c + 1) * 128], idt[:]))
                return insts[-1]
            e_htr = p.step("tensor", htr, wait=[e_rh, last_psTR[0]])
            ev_htr[(e, sub)] = e_htr
            def evh(v, par=par):
                src = psTR[:, 0:HEXP].rearrange("p (c q) -> p c q", q=128)
                v.tensor_copy(hT[par][:], src)
                return v.tensor_copy(jnk[:, 0:NHC],
                                     hT[par][:, :, 127:128]
                                     .rearrange("p c q -> p (c q)"))
            e_evh = p.step("vector", evh, wait=[e_htr])
            last_psTR[0] = e_evh

            def ymm(t_, par=par, wp=wp):
                last = None
                for n in range(2):
                    t_.matmul(psE[:, n * 512:(n + 1) * 512], ones1f[:],
                              b2sb[wp][:, n * 512:(n + 1) * 512],
                              start=True, stop=False)
                    for c in range(NHC):
                        t_.matmul(psE[:, n * 512:(n + 1) * 512],
                                  hT[par][:, c, :],
                                  w2sb[wp][:, c, n * 512:(n + 1) * 512],
                                  start=False, stop=False)
                    last = t_.matmul(psE[:, n * 512:(n + 1) * 512], zrowf[:],
                                     b2sb[wp][:, n * 512:(n + 1) * 512],
                                     start=False, stop=True)
                return last
            e_ymm = p.step("tensor", ymm,
                           wait=[e_evh,
                                 ev_reluy.get((e, sub - 1)) or
                                 ev_reluy.get((e - 1, NSUB - 1))])
            ev_ymm[(e, sub)] = e_ymm
            e_ry = p.step("scalar",
                          lambda sc, par=par: sc.activation(ysb[par][:],
                                                            psE[:], AF.Relu),
                          wait=[e_ymm,
                                ev_yst.get((e, sub - 2)) or
                                ev_yst.get((e - 1, NSUB - 2 + (sub % 2)))])
            ev_reluy[(e, sub)] = e_ry
            ev_yst[(e, sub)] = p.step(
                "sync",
                lambda s, row0=row0, par=par: s.dma_start(
                    out=yg_h[row0:row0 + 128, :], in_=ysb[par][:]),
                wait=[e_ry], dma=True)

    ev_g = {}
    ev_xr = {}
    ev_res = {}
    ev_sto = {}
    e_elast = ev_yst[(NE - 1, NSUB - 1)]
    for t in range(T):
        par = t % 2
        gw = [e_elast, ev_res.get(t - 2)]
        ev_g[t] = p.step(
            "gpsimd",
            lambda g, t=t, par=par: g.indirect_dma_start(
                out=yt_[par][:], out_offset=None,
                in_=yg_h[0:128, :],
                in_offset=bass.IndirectOffsetOnAxis(idxt2[:, t:t + 1], 0)),
            wait=gw, dma=True)
        ev_xr[t] = p.step("sync",
                          lambda s, t=t, par=par: s.dma_start(
                              out=xtl[par][:], in_=x[t * 128:(t + 1) * 128, :]),
                          wait=[ev_res.get(t - 2)], dma=True)
        def res(v, par=par):
            return v.scalar_tensor_tensor(xot[par][:], yt_[par][:], 0.5,
                                          xtl[par][:], ALU.mult, ALU.add)
        ev_res[t] = p.step("vector", res,
                           wait=[ev_g[t], ev_xr[t], ev_sto.get(t - 2)])
        ev_sto[t] = p.step("sync",
                           lambda s, t=t, par=par: s.dma_start(
                               out=y[t * 128:(t + 1) * 128, :],
                               in_=xot[par][:]),
                           wait=[ev_res[t]], dma=True)
    sbctx.close()
    return ev_sto[T - 1]


def build_moe_top1(NTOK=4096):
    nc = bass.Bass("TRN2", target_bir_lowering=False)
    ctx = ExitStack()
    io = {}
    io["x"] = nc.dram_tensor("x", [NTOK, D], F32, kind="ExternalInput").ap()
    io["y"] = nc.dram_tensor("y", [NTOK, D], F32, kind="ExternalOutput").ap()
    io["rw"] = nc.dram_tensor("rw", [32, D], F32, kind="ExternalInput").ap()
    io["rb"] = nc.dram_tensor("rb", [32], F32, kind="ExternalInput").ap()
    io["w1"] = nc.dram_tensor("w1", [32, D, 256], F32, kind="ExternalInput").ap()
    io["b1"] = nc.dram_tensor("b1", [32, 256], F32, kind="ExternalInput").ap()
    io["w2"] = nc.dram_tensor("w2", [32, 256, D], F32, kind="ExternalInput").ap()
    io["b2"] = nc.dram_tensor("b2", [32, D], F32, kind="ExternalInput").ap()
    io["ident"] = nc.dram_tensor("ident", [128, 128], F32,
                                 kind="ExternalInput").ap()
    io["ecol"] = nc.dram_tensor("ecol", [32, 1], F32, kind="ExternalInput").ap()
    io["dbg_idx"] = nc.dram_tensor("dbg_idx", [128, NTOK // 128], F32,
                                   kind="ExternalOutput").ap()
    io["dbg_xg"] = nc.dram_tensor("dbg_xg", [32 * 256, D], BF16,
                                  kind="ExternalOutput").ap()
    io["dbg_idx2"] = nc.dram_tensor("dbg_idx2", [128, NTOK // 128], F32,
                                    kind="ExternalOutput").ap()
    io["dbg_hmt"] = nc.dram_tensor("dbg_hmt", [32, NTOK], F32,
                                   kind="ExternalOutput").ap()
    io["dbg_cum"] = nc.dram_tensor("dbg_cum", [32, NTOK], F32,
                                   kind="ExternalOutput").ap()
    io["dbg_yg"] = nc.dram_tensor("dbg_yg", [32 * 256, D], F32,
                                  kind="ExternalOutput").ap()
    p = Prog(nc, ctx)
    ev = emit_moe_top1(nc, ctx, p, io, NTOK=NTOK)
    p.wait_only("sync", [ev])
    p.build()
    ctx.close()
    return nc


from concourse.bass_utils import run_bass_kernel_spmd

_N_CORES = 8
_S = 4096
_NH, _HD = 16, 64


def _consts():
    ident = np.eye(128, dtype=np.float32)
    inv = 1.0 / 10000.0 ** (np.arange(0, _HD, 2, dtype=np.float64) / _HD)
    pos = np.arange(_S, dtype=np.float64)
    emb = np.repeat(pos[:, None] * inv[None, :], 2, axis=-1)
    cos = np.tile(np.cos(emb), (1, _NH)).astype(np.float32)
    sin = np.tile(np.sin(emb), (1, _NH)).astype(np.float32)
    import ml_dtypes
    cm = np.zeros((128, 4, 512), np.float32)
    for off in range(4):
        for kr in range(128):
            cm[kr, off, off * 128 + kr:] = 1.0
    cmask = cm.astype(ml_dtypes.bfloat16)
    return ident, cos, sin, cmask


def kernel(**inputs):
    f = lambda k: np.ascontiguousarray(np.asarray(inputs[k], dtype=np.float32))
    x = f("L0_attn_x")
    ident, cos, sin, cmask = _consts()
    nc = build_fused(NTOK=_S)
    shared = dict(ident=ident, cos=cos, sin=sin, cmask=cmask,
                  ecol=(np.arange(32, dtype=np.float32) * 256)[:, None],
                  rw=f("L0_ffn_router_w"), rb=f("L0_ffn_router_b"),
                  w1=f("L0_ffn_weights1"), b1=f("L0_ffn_bias1"),
                  w2=f("L0_ffn_weights2"), b2=f("L0_ffn_bias2"),
                  Wq0=f("L0_attn_Wq"), Wk0=f("L0_attn_Wk"),
                  Wv0=f("L0_attn_Wv"), Wo0=f("L0_attn_Wo"),
                  wg=f("L1_ffn_w_gate"), wu=f("L1_ffn_w_up"),
                  wd=f("L1_ffn_w_down"),
                  Wq1=f("L1_attn_W_q"), bq1=f("L1_attn_b_q"),
                  Wk1=f("L1_attn_W_k"), bk1=f("L1_attn_b_k"),
                  Wv1=f("L1_attn_W_v"), bv1=f("L1_attn_b_v"),
                  Wo1=f("L1_attn_W_out"), bo1=f("L1_attn_b_out"))
    maps = [dict(x=np.ascontiguousarray(x[i]), **shared)
            for i in range(_N_CORES)]
    res = run_bass_kernel_spmd(nc, maps, core_ids=list(range(_N_CORES)))
    return np.stack([np.asarray(r["y"]) for r in res.results]).astype(np.float32)
